# version 30
# baseline (speedup 1.0000x reference)
import numpy as np
import ml_dtypes
from contextlib import ExitStack

import concourse.bass as bass
import concourse.mybir as mybir
from concourse.bass_utils import run_bass_kernel_spmd

F32 = mybir.dt.float32
BF16 = mybir.dt.bfloat16
F16 = mybir.dt.float16
I8 = mybir.dt.int8
ALU = mybir.AluOpType
ACT = mybir.ActivationFunctionType
AX = mybir.AxisListType.X

P = 128
NT = 16
D = 1024
KC = 8
NOB = 8
T = NT * P
MAGIC = 12582912.0
LN_EPS = 1e-5
RMS_EPS = 1e-6
Q_EPS = 1e-6
L2_EPS = 1e-12

ENGINES = ("Y", "T", "V", "S", "G")
DMA_SEMS = tuple("XIN%d" % i for i in range(16)) + (
    "WQ", "WKV", "SM", "TRA0", "TRA1", "TRA2", "TRA3", "TRB0", "TRB1",
    "TRB2", "TRB3", "VTR", "OUT")
ALL_SEMS = ENGINES[1:] + DMA_SEMS


class Tok:
    __slots__ = ("sem", "cnt", "stream")

    def __init__(self, sem, cnt, stream=None):
        self.sem = sem
        self.cnt = cnt
        self.stream = stream


class Planner:
    def __init__(self):
        self.items = {e: [] for e in ENGINES}
        self.counts = {s: 0 for s in ALL_SEMS}
        self._clock = {e: {s: 0 for s in ALL_SEMS} for e in ENGINES}
        self._snap = {s: {0: {t: 0 for t in ALL_SEMS}} for s in ALL_SEMS}
        self.n_waits = 0

    def add(self, eng, builder, deps=(), inc=1, sem=None):
        deps = [d for d in deps if d is not None]
        if sem is None:
            sem = eng
        cl = self._clock[eng]
        waits = []
        for d in deps:
            assert isinstance(d, Tok), d
            assert d.cnt <= self.counts[d.sem], "dep on future token"
            if cl.get(d.sem, 0) >= d.cnt:
                continue
            waits.append((d.sem, d.cnt))
            snap = self._best_snap(d.sem, d.cnt)
            for f in ALL_SEMS:
                if snap[f] > cl[f]:
                    cl[f] = snap[f]
            cl[d.sem] = max(cl[d.sem], d.cnt)
        self.n_waits += len(waits)
        self.items[eng].append((waits, builder, sem if inc else None, inc))
        if inc:
            self.counts[sem] += inc
            self._snap[sem][self.counts[sem]] = dict(cl)
        return Tok(sem, self.counts[sem], eng)

    def _best_snap(self, sem, cnt):
        snaps = self._snap[sem]
        if cnt in snaps:
            return snaps[cnt]
        best = 0
        for c in snaps:
            if best < c <= cnt:
                best = c
        return snaps[best]

    def emit(self, eng_name, eng_obj, sems):
        for waits, builder, inc_sem, inc in self.items[eng_name]:
            for (ws, wc) in waits:
                eng_obj.wait_ge(sems[ws], wc)
            inst = builder(eng_obj)
            if inc:
                inst.then_inc(sems[inc_sem], inc)


def build_program(n_sub=3, parts=('vec', 'et', 'wff', 'wqkv', 'wout'), debug=False):
    nc = bass.Bass("TRN2", target_bir_lowering=False,
                   detect_race_conditions=False)

    x_in = nc.dram_tensor("x", [T, D], F32, kind="ExternalInput").ap()
    wqT_in = nc.dram_tensor("wqT", [P, KC * D], I8,
                            kind="ExternalInput").ap()
    cols_in = nc.dram_tensor("cols_in", [P, 24], F32, kind="ExternalInput").ap()
    wqkvT_in = nc.dram_tensor("wqkvT", [P, KC * 384], F32, kind="ExternalInput").ap()
    woutT_in = nc.dram_tensor("woutT", [P, D], F32, kind="ExternalInput").ap()
    temp_in = nc.dram_tensor("temp", [8], F32, kind="ExternalInput").ap()
    auxf_in = nc.dram_tensor("auxf", [P, P], F32, kind="ExternalInput").ap()
    auxb_in = nc.dram_tensor("auxb", [P, 2 * P], F16, kind="ExternalInput").ap()
    out_d = nc.dram_tensor("out", [T, D], F32, kind="ExternalOutput").ap()
    dbg_d = {}

    with ExitStack() as st:
        def sb(name, shape, dtype):
            return st.enter_context(nc.sbuf_tensor(name, shape, dtype))

        def psalloc(name, shape, dtype=F32):
            return st.enter_context(nc.psum_tensor(name, shape, dtype))

        X = sb("X", [P, NT * D], F32)
        WffT = sb("WffT", [P, KC * D], F16)
        WqkvT = sb("WqkvT", [P, KC * 384], F16)
        WoutT = sb("WoutT", [P, D], F16)
        arena = sb("arena", [P, NOB * T], F16)
        e_tm = sb("e_tm", [P, NT * D], F16)
        qdTa = sb("qdTa", [P, KC * 1536], F16)
        attn_fm = sb("attn_fm", [P, T], F16)
        v_scr = sb("v_scr", [P, 3 * D], F32)
        q8r = sb("q8r", [P, 2 * D], I8)
        qdr = sb("qdr", [P, 4 * D], F16)
        xnr = qdr
        stats = sb("stats", [P, 15 * NT], F32)
        Gq = sb("Gq", [P, 256], F32)
        Gkk = sb("Gkk", [P, P], F32)
        sim1 = sb("sim1", [P, P], F32)
        sim2 = sb("sim2", [P, P], F32)
        Pbf = sb("Pbf", [P, P], F16)
        PTb = sb("PTb", [P, P], F16)
        dkb = sb("dkb", [P, P], F16)
        maskb = sb("maskb", [P, P], F16)
        identb = sb("identb", [P, P], F16)
        identf = sb("identf", [P, P], F32)
        onesh = sb("onesh", [1, P], F16)
        cols = sb("cols", [P, 40], F32)
        smallv = sb("smallv", [P, 16], F32)

        psA = psalloc("psA", [P, 1024], F32)
        psB = psalloc("psB", [P, 1024], F32)
        psC = psalloc("psC", [P, 1024], F32)
        ps_tb = psalloc("ps_tb", [P, 512], F16)
        ps_tf = psalloc("ps_tf", [P, 512], F32)

        sems = {e: st.enter_context(nc.semaphore("sem_" + e))
                for e in ALL_SEMS}
        block = st.enter_context(nc.Block())

        pl = Planner()

        qdTa3 = qdTa[:].rearrange("c (kc t) -> c kc t", kc=KC)
        etm3 = e_tm[:].rearrange("t (tc o) -> t tc o", tc=NT)
        A_QK = 0
        A_V = NT * 256
        A_VF = A_V + NT * P

        SUMS, SUMSQ, MEAN, SCOL, AMAX, INVXS, XSC, TMP1, TMP2, TMP3 = \
            (i * NT for i in range(10))
        CDOL = 10 * NT
        ETCOL = 11 * NT
        MAXC, MINC = AMAX, INVXS
        ACOL, BCOL, EPSC = 12 * NT, 13 * NT, 14 * NT
        C_WS, C_BI, C_NW, C_WSH, C_BH = 0, 8, 16, 24, 32
        DMA = 16

        def col(base, k):
            return stats[:, base + k: base + k + 1]

        def slab_sl(base, s):
            return stats[:, base + 4 * s: base + 4 * (s + 1)]

        temp_ld = pl.add("S", lambda e: e.dma_start(
            stats[0:1, ETCOL:ETCOL + 8],
            temp_in[:].rearrange("(a t) -> a t", a=1)),
            inc=DMA, sem="SM")
        identf_t = pl.add("S", lambda e: e.dma_start(
            identf[:, :], auxf_in[:, 0:P]), inc=DMA, sem="SM")
        mask_t = pl.add("S", lambda e: e.dma_start(
            maskb[:, :], auxb_in[:, P:2 * P]), inc=DMA, sem="SM")
        identb_t = pl.add("S", lambda e: e.dma_start(
            identb[:, :], auxb_in[:, 0:P]), inc=DMA, sem="SM")
        cols_ld = pl.add("S", lambda e: e.dma_start(
            cols[:, C_WS:C_WS + 24], cols_in[:, :]), inc=DMA, sem="SM")
        stg_i8 = e_tm[:, 12288:16384].bitcast(I8)
        stg_f32 = e_tm[:, 6144:12288].bitcast(F32)
        stg_wo = e_tm[:, 4096:6144].bitcast(F32)
        wout_ld = pl.add("S", lambda e: e.dma_start(
            stg_wo[:, 0:1024], woutT_in[:, :]), inc=DMA, sem="SM")
        sm_all = wout_ld
        temp_ld = identf_t = mask_t = identb_t = cols_ld = sm_all
        ws_ld = bi_ld = nw_ld = sm_all

        wq_lds = pl.add("S", lambda e: e.dma_start(
            stg_i8[:, :], wqT_in[:, :]), inc=DMA, sem="WQ")
        wff_cvs = []
        for h in range(2):
            cv = pl.add("V", (lambda h: lambda e: e.tensor_copy(
                WffT[:, h * 4 * D:(h + 1) * 4 * D],
                stg_i8[:, h * 4 * D:(h + 1) * 4 * D]))(h),
                deps=([wq_lds] if h == 0 else []))
            wff_cvs.append(cv)
        wfft_done = wff_cvs[-1]

        wkv_lds = pl.add("S", lambda e: e.dma_start(
            stg_f32[:, :], wqkvT_in[:, :]), inc=DMA, sem="WKV")
        wqkv_cvs = []
        for kc in range(KC):
            cv = pl.add("V", (lambda kc: lambda e: e.tensor_scalar(
                WqkvT[:, kc * 384:(kc + 1) * 384],
                stg_f32[:, kc * 384:(kc + 1) * 384],
                cols[:, C_NW + kc:C_NW + kc + 1], None,
                ALU.mult))(kc),
                deps=([wkv_lds, nw_ld] if kc == 0 else []))
            wqkv_cvs.append(cv)
        wqkvT_done = wqkv_cvs[-1]
        woutT_done = pl.add("V", lambda e: e.tensor_copy(
            WoutT[:, :], stg_wo[:, 0:1024]), deps=[wout_ld])

        xin = []
        for k in range(NT):
            xin.append(pl.add("Y", (lambda k: lambda e: e.dma_start(
                X[:, k * D:(k + 1) * D], x_in[k * P:(k + 1) * P, :]))(k),
                deps=([wq_lds] if k == 4 else []),
                inc=DMA, sem="XIN%d" % k))

        epsc_t = pl.add("V", lambda e: e.memset(
            stats[:, EPSC:EPSC + NT], Q_EPS))
        onesf_t = pl.add("V", lambda e: e.memset(onesh[:, :], 1.0))
        wsh_t = pl.add("V", lambda e: e.tensor_scalar(
            cols[:, C_WSH:C_WSH + 8], cols[:, C_WS:C_WS + 8], 0.5, None,
            ALU.mult), deps=[ws_ld])
        bh_t = pl.add("V", lambda e: e.tensor_scalar(
            cols[:, C_BH:C_BH + 8], cols[:, C_BI:C_BI + 8], 0.5, None,
            ALU.mult), deps=[bi_ld])

        ET = 8
        et_exp = pl.add("S", lambda e: e.activation(
            stats[0:1, ETCOL + 8:ETCOL + 16], stats[0:1, ETCOL:ETCOL + 8],
            ACT.Exp), deps=[temp_ld])
        et_bf = pl.add("V", lambda e: e.tensor_copy(
            Pbf[0:1, 0:8], stats[0:1, ETCOL + 8:ETCOL + 16]),
            deps=[et_exp])
        et_mm = pl.add("T", lambda e: e.matmul(
            psC[:, 0:8], onesh[0:1, :], Pbf[0:1, 0:8], start=True,
            stop=True), deps=[et_bf, onesf_t])
        et_ev = pl.add("V", lambda e: e.tensor_copy(
            stats[:, ETCOL:ETCOL + 8], psC[:, 0:8]), deps=[et_mm])
        et_s1 = pl.add("G", lambda e: e.affine_select(
            stats[:, ETCOL + 8:ETCOL + 16], stats[:, ETCOL:ETCOL + 8],
            [[-16, 8]], ALU.is_ge, 0.0, base=0, channel_multiplier=1),
            deps=[et_ev])
        et_s2 = pl.add("G", lambda e: e.affine_select(
            stats[:, ETCOL:ETCOL + 8], stats[:, ETCOL + 8:ETCOL + 16],
            [[16, 8]], ALU.is_ge, 0.0, base=15, channel_multiplier=-1),
            deps=[et_s1])
        et_col = pl.add("V", lambda e: e.reduce_sum(
            smallv[:, ET:ET + 1], stats[:, ETCOL:ETCOL + 8], AX),
            deps=[et_s2])

        def ffn(sub, x_ready, x_s_reader, sums_ready, prev_etm_reader,
                prev_qdT_reader, prev_arena_reader, init_vscr, init_q8,
                init_qdr):
            sq_t = [None] * NT
            sum_t = [None] * NT
            y_t = [None] * NT
            am_t = [None] * NT
            tra_t = [None] * NT
            vscr_user = list(init_vscr)
            q8_free = [init_q8, init_q8]
            qdr_free = list(init_qdr) + [None] * (4 - len(init_qdr))
            fm_last = [None] * 4
            evac_t = {}
            if sub == 1:
                fm_slots = [psB[:, 0:512], psB[:, 512:1024],
                            psA[:, 0:512], psA[:, 512:1024]]
            else:
                fm_slots = [psB[:, 0:512], psB[:, 512:1024]]
            n_slots = len(fm_slots)
            slot_free = [prev_arena_reader] * n_slots

            def phase1_emit(g):
                for k in range(4 * g, 4 * g + 4):
                    sl = 2
                    deps_sq = [x_ready[k], vscr_user[2]]
                    if sub == 1:
                        sum_t[k] = pl.add("S", (lambda k: lambda e:
                            e.activation(
                                v_scr[:, 2 * D:3 * D], X[:, k * D:(k + 1) * D],
                                ACT.Identity, accum_out=col(SUMS, k)))(k),
                            deps=[x_ready[k], vscr_user[2]])
                        deps_sq.append(sum_t[k])
                    if sums_ready is not None:
                        deps_sq.append(sums_ready[k])
                    sq_t[k] = pl.add("S", (lambda k, sl: lambda e: e.activation(
                        v_scr[:, sl * D:(sl + 1) * D], X[:, k * D:(k + 1) * D],
                        ACT.Square, accum_out=col(SUMSQ, k)))(k, sl),
                        deps=deps_sq)
                    vscr_user[2] = sq_t[k]
                    x_s_reader[k] = sq_t[k]
                    am_t[k] = pl.add("V", (lambda k: lambda e: e.reduce_max(
                        col(MAXC, k), X[:, k * D:(k + 1) * D], AX))(k),
                        deps=[x_ready[k]])
                    y_t[k] = pl.add("V", (lambda k: lambda e: e.tensor_reduce(
                        col(MINC, k), X[:, k * D:(k + 1) * D], AX,
                        ALU.min))(k), deps=[x_ready[k]])

            math2_g = [None] * 4

            def phase2_emit(g):
                def R(base, g=g):
                    return stats[:, base + 4 * g: base + 4 * g + 4]

                ks = range(4 * g, 4 * g + 4)
                d1 = [sq_t[k] for k in ks] + [am_t[k] for k in ks] + \
                    [y_t[k] for k in ks] + \
                    ([sum_t[k] for k in ks] if sub == 1 else [])
                t = pl.add("V", lambda e, R=R: e.tensor_scalar(
                    R(MEAN), R(SUMS), 1.0 / D, None, ALU.mult), deps=d1)
                t = pl.add("V", lambda e, R=R: e.tensor_scalar(
                    R(TMP1), R(SUMSQ), 1.0 / D, None, ALU.mult), deps=[t])
                t = pl.add("S", lambda e, R=R: e.activation(
                    R(TMP2), R(MEAN), ACT.Square), deps=[t])
                t = pl.add("V", lambda e, R=R: e.tensor_tensor(
                    R(TMP1), R(TMP1), R(TMP2), ALU.subtract), deps=[t])
                t = pl.add("V", lambda e, R=R: e.tensor_scalar(
                    R(TMP1), R(TMP1), LN_EPS, None, ALU.add), deps=[t])
                t = pl.add("V", lambda e, R=R: e.reciprocal(R(TMP2), R(TMP1)),
                           deps=[t])
                t = pl.add("S", lambda e, R=R: e.activation(
                    R(SCOL), R(TMP2), ACT.Sqrt), deps=[t])
                t = pl.add("V", lambda e, R=R: e.tensor_tensor(
                    R(TMP1), R(MAXC), R(MEAN), ALU.subtract), deps=[t])
                t = pl.add("V", lambda e, R=R: e.tensor_tensor(
                    R(TMP2), R(MEAN), R(MINC), ALU.subtract), deps=[t])
                t = pl.add("V", lambda e, R=R: e.tensor_tensor(
                    R(TMP1), R(TMP1), R(TMP2), ALU.max), deps=[t])
                t = pl.add("V", lambda e, R=R: e.tensor_tensor(
                    R(TMP1), R(TMP1), R(SCOL), ALU.mult), deps=[t])
                t = pl.add("V", lambda e, R=R: e.tensor_tensor(
                    R(TMP1), R(TMP1), R(EPSC), ALU.max), deps=[t, epsc_t])
                t = pl.add("V", lambda e, R=R: e.tensor_scalar(
                    R(XSC), R(TMP1), 1.0 / 127.0, None, ALU.mult), deps=[t])
                t = pl.add("V", lambda e, R=R: e.reciprocal(R(TMP2), R(TMP1)),
                           deps=[t])
                t = pl.add("V", lambda e, R=R: e.tensor_scalar(
                    R(TMP2), R(TMP2), 127.0, None, ALU.mult), deps=[t])
                t = pl.add("V", lambda e, R=R: e.tensor_tensor(
                    R(ACOL), R(SCOL), R(TMP2), ALU.mult), deps=[t])
                t = pl.add("V", lambda e, R=R: e.tensor_tensor(
                    R(BCOL), R(MEAN), R(ACOL), ALU.mult), deps=[t])
                math2_g[g] = pl.add("V", lambda e, R=R: e.tensor_scalar(
                    R(BCOL), R(BCOL), -1.0, None, ALU.mult), deps=[t])

            def quant_emit(s):
                for j in range(4):
                    k = 4 * s + j
                    sl = k % 2
                    sq8 = k % 2
                    t = pl.add("V", (lambda k, sl: lambda e: e.tensor_scalar(
                        v_scr[:, sl * D:(sl + 1) * D], X[:, k * D:(k + 1) * D],
                        col(MEAN, k), col(ACOL, k), ALU.subtract,
                        ALU.mult))(k, sl),
                        deps=[math2_g[s], vscr_user[sl], sq_t[k]])
                    t = pl.add("V", (lambda k, sl, sq8: lambda e: e.tensor_scalar(
                        q8r[:, sq8 * D:(sq8 + 1) * D],
                        v_scr[:, sl * D:(sl + 1) * D],
                        MAGIC, MAGIC, ALU.add, ALU.subtract))(k, sl, sq8),
                        deps=[t, q8_free[sq8]])
                    vscr_user[sl] = t
                    deq = pl.add("S", (lambda k, sq8: lambda e: e.activation(
                        qdr[:, (k % 4) * D:(k % 4 + 1) * D],
                        q8r[:, sq8 * D:(sq8 + 1) * D],
                        ACT.Identity, scale=col(XSC, k)))(k, sq8),
                        deps=[t, qdr_free[k % 4]])
                    q8_free[sq8] = deq
                    slot_dep = prev_qdT_reader if k < 12 else \
                        fm_last[(k - 12) // 4]
                    kslot = (k // 4) % 3 * 4 + k % 4
                    tra_t[k] = pl.add("Y", (lambda k, kslot: lambda e:
                        e.dma_start(
                            qdTa3[:, :, kslot * P:(kslot + 1) * P],
                            qdr[:, (k % 4) * D:(k % 4 + 1) * D],
                            transpose=True))(k, kslot),
                        deps=[deq, slot_dep], inc=DMA,
                        sem="TRA%d" % (k % 4))
                    qdr_free[k % 4] = tra_t[k]

            def fm_emit(s):
                slab_deps = [tra_t[4 * s + j] for j in range(4)] + [wfft_done]
                for ob in range(NOB):
                    slot = (s * NOB + ob) % n_slots
                    dd = slab_deps + [slot_free[slot]]
                    tok = None
                    for kc in range(KC):
                        tok = pl.add("T", (lambda s, ob, kc, slot: lambda e:
                            e.matmul(
                                fm_slots[slot],
                                WffT[:, kc * D + ob * P: kc * D + (ob + 1) * P],
                                qdTa3[:, kc,
                                      (s % 3) * 512:(s % 3) * 512 + 512],
                                start=(kc == 0),
                                stop=(kc == KC - 1)))(s, ob, kc, slot),
                            deps=(dd if kc == 0 else []),
                            inc=(1 if kc == KC - 1 else 0))
                    fm_last[s] = tok
                    ev = pl.add("S", (lambda s, ob, slot: lambda e: e.activation(
                        arena[:, ob * T + s * 512: ob * T + (s + 1) * 512],
                        fm_slots[slot], ACT.Identity,
                        scale=cols[:, C_WSH + ob:C_WSH + ob + 1],
                        bias=cols[:, C_BH + ob:C_BH + ob + 1]))(s, ob, slot),
                        deps=[tok, wsh_t, bh_t, prev_arena_reader])
                    evac_t[(s, ob)] = ev
                    slot_free[slot] = ev

            for g in range(4):
                phase1_emit(g)
            for g in range(4):
                phase2_emit(g)
            trb_t = {}

            def trb_emit(q):
                for ob in range(NOB):
                    dd = [evac_t[(q, ob)], prev_etm_reader]
                    trb_t[(q, ob)] = pl.add("S", (lambda q, ob: lambda e:
                        e.dma_start(
                            etm3[:, q * 4:(q + 1) * 4, ob * P:(ob + 1) * P],
                            arena[:, ob * T + q * 512: ob * T + (q + 1) * 512],
                            transpose=True))(q, ob), inc=DMA, deps=dd,
                        sem="TRB%d" % q)

            quant_emit(0)
            for g in range(4):
                if g < 3:
                    quant_emit(g + 1)
                fm_emit(g)
                trb_emit(g)

            xplus = [None] * NT
            trb_last = {q: max((trb_t[(q, ob)] for ob in range(NOB)),
                               key=lambda t: t.cnt) for q in range(4)}
            for k in range(NT):
                h = k // 4
                dd = [trb_last[h]]
                dd += [sq_t[k]]
                if sub == 1:
                    xplus[k] = pl.add("V", (lambda k: lambda e:
                        e.scalar_tensor_tensor(
                            X[:, k * D:(k + 1) * D], e_tm[:, k * D:(k + 1) * D],
                            0.0, X[:, k * D:(k + 1) * D], ALU.add, ALU.add,
                            accum_out=col(SUMS, k)))(k), deps=dd)
                else:
                    xplus[k] = pl.add("V", (lambda k: lambda e:
                        e.scalar_tensor_tensor(
                            X[:, k * D:(k + 1) * D], e_tm[:, k * D:(k + 1) * D],
                            0.0, X[:, k * D:(k + 1) * D], ALU.add,
                            ALU.add))(k), deps=dd)
            return dict(
                xplus=xplus,
                etm_reader=xplus[-1],
                qdT_reader=Tok("T", pl.counts["T"], "T"),
                arena_reader=trb_t[(1, NOB - 1)],
                vscr_user=vscr_user,
                q8_free=q8_free[0 if q8_free[0].cnt > q8_free[1].cnt else 1],
                qdr_free=qdr_free,
                tra_last=tra_t[-1],
            )

        def attn(x_ready, x_s_reader, prev_qdT_reader, prev_arena_reader,
                 init_vscr, init_xnr):
            sq_t = [None] * NT
            xn_t = [None] * NT
            tra_t = [None] * NT
            qkv_mm = [None] * NT
            qkv_ev = [None] * NT
            xnr_free = list(init_xnr)
            vscr_user = list(init_vscr)
            qkv_slots = [psB[:, 0:384], psB[:, 512:512 + 384]]
            vtr_t = []

            math_c_g = [None] * 4

            def aph1_emit(g):
                for k in range(4 * g, 4 * g + 4):
                    sl = 2
                    sq_t[k] = pl.add("S", (lambda k, sl: lambda e: e.activation(
                        v_scr[:, sl * D:(sl + 1) * D], X[:, k * D:(k + 1) * D],
                        ACT.Square, accum_out=col(SUMSQ, k)))(k, sl),
                        deps=[x_ready[k], vscr_user[2]])
                    vscr_user[2] = sq_t[k]
                    x_s_reader[k] = sq_t[k]

            def amath_emit(g):
                def R(base, g=g):
                    return stats[:, base + 4 * g: base + 4 * g + 4]

                ks = range(4 * g, 4 * g + 4)
                d1 = [sq_t[k] for k in ks] + [x_ready[k] for k in ks]
                t = pl.add("V", lambda e, R=R: e.tensor_scalar(
                    R(MEAN), R(SUMS), 1.0 / D, None, ALU.mult), deps=d1)
                t = pl.add("V", lambda e, R=R: e.tensor_scalar(
                    R(TMP1), R(SUMSQ), 1.0 / D, None, ALU.mult), deps=[t])
                t = pl.add("S", lambda e, R=R: e.activation(
                    R(TMP2), R(MEAN), ACT.Square), deps=[t])
                t = pl.add("V", lambda e, R=R: e.tensor_tensor(
                    R(TMP1), R(TMP1), R(TMP2), ALU.subtract), deps=[t])
                t = pl.add("V", lambda e, R=R: e.tensor_scalar(
                    R(TMP2), R(TMP1), LN_EPS, None, ALU.add), deps=[t])
                t = pl.add("V", lambda e, R=R: e.reciprocal(R(TMP3), R(TMP2)),
                           deps=[t])
                t_s = pl.add("S", lambda e, R=R: e.activation(
                    R(SCOL), R(TMP3), ACT.Sqrt), deps=[t])
                t = pl.add("V", lambda e, R=R: e.tensor_tensor(
                    R(TMP2), R(TMP1), R(TMP3), ALU.mult), deps=[t_s])
                t = pl.add("V", lambda e, R=R: e.tensor_scalar(
                    R(TMP2), R(TMP2), RMS_EPS, None, ALU.add), deps=[t])
                t = pl.add("V", lambda e, R=R: e.reciprocal(R(TMP1), R(TMP2)),
                           deps=[t])
                t = pl.add("S", lambda e, R=R: e.activation(
                    R(TMP2), R(TMP1), ACT.Sqrt), deps=[t])
                math_c_g[g] = pl.add("V", lambda e, R=R: e.tensor_tensor(
                    R(CDOL), R(SCOL), R(TMP2), ALU.mult), deps=[t])

            def axn_emit(g):
                for k in range(4 * g, 4 * g + 4):
                    xn_t[k] = pl.add("V", (lambda k: lambda e: e.tensor_scalar(
                        xnr[:, (k % 4) * D:(k % 4 + 1) * D],
                        X[:, k * D:(k + 1) * D],
                        col(MEAN, k), col(CDOL, k), ALU.subtract,
                        ALU.mult))(k),
                        deps=[math_c_g[k // 4], xnr_free[k % 4]])
                    tra_t[k] = pl.add("S", (lambda k: lambda e: e.dma_start(
                        qdTa3[:, :, (k % 8) * P:(k % 8 + 1) * P],
                        xnr[:, (k % 4) * D:(k % 4 + 1) * D],
                        transpose=True))(k),
                        deps=[xn_t[k],
                              prev_qdT_reader if k < 8 else qkv_mm[k - 8]],
                        inc=DMA, sem="TRA%d" % (k % 4))
                    xnr_free[k % 4] = tra_t[k]

            def aqkv_emit(g):
                for k in range(4 * g, 4 * g + 4):
                    slot = k % 2
                    tok = None
                    dd0 = [tra_t[k], wqkvT_done]
                    dd0.append(qkv_ev[k - 2] if k >= 2 else prev_arena_reader)
                    for kc in range(KC):
                        tok = pl.add("T", (lambda k, kc, slot: lambda e:
                            e.matmul(
                                qkv_slots[slot],
                                qdTa3[:, kc, (k % 8) * P:(k % 8 + 1) * P],
                                WqkvT[:, kc * 384:(kc + 1) * 384],
                                start=(kc == 0),
                                stop=(kc == KC - 1)))(k, kc, slot),
                            deps=(dd0 if kc == 0 else []),
                            inc=(1 if kc == KC - 1 else 0))
                    qkv_mm[k] = tok
                    ev1 = pl.add("V", (lambda k, slot: lambda e: e.tensor_copy(
                        arena[:, A_QK + k * 256: A_QK + (k + 1) * 256],
                        qkv_slots[slot][:, 0:256]))(k, slot),
                        deps=[tok, prev_arena_reader])
                    qkv_ev[k] = pl.add("V", (lambda k, slot: lambda e:
                        e.tensor_copy(
                            arena[:, A_V + k * P: A_V + (k + 1) * P],
                            qkv_slots[slot][:, 256:384]))(k, slot),
                        deps=[ev1])
                    gq_t[0] = pl.add("T", (lambda k: lambda e: e.matmul(
                        psC[:, 0:256],
                        arena[:, A_QK + k * 256: A_QK + k * 256 + P],
                        arena[:, A_QK + k * 256: A_QK + (k + 1) * 256],
                        start=(k == 0), stop=(k == NT - 1)))(k),
                        deps=([ev1, et_ev] if k == 0 else [ev1]),
                        inc=(1 if k == NT - 1 else 0))
                    gk_t[0] = pl.add("T", (lambda k: lambda e: e.matmul(
                        psC[:, 512:640],
                        arena[:, A_QK + k * 256 + P: A_QK + (k + 1) * 256],
                        arena[:, A_QK + k * 256 + P: A_QK + (k + 1) * 256],
                        start=(k == 0), stop=(k == NT - 1)))(k),
                        deps=[], inc=(1 if k == NT - 1 else 0))
                    vtr_t.append(pl.add("Y", (lambda k: lambda e: e.dma_start(
                        arena[:, A_VF + k * P: A_VF + (k + 1) * P],
                        arena[:, A_V + k * P: A_V + (k + 1) * P],
                        transpose=True))(k), deps=[qkv_ev[k]], inc=DMA,
                        sem="VTR"))

            gq_t = [None]
            gk_t = [None]
            for g in range(4):
                aph1_emit(g)
                amath_emit(g)
                axn_emit(g)
                aqkv_emit(g)
            gq_tok = gq_t[0]
            gk_tok = gk_t[0]
            vtr_all = [vtr_t[-1]]

            gq_ev = pl.add("V", lambda e: e.tensor_copy(
                Gq[:, :], psC[:, 0:256]), deps=[gq_tok])
            gk_ev = pl.add("V", lambda e: e.tensor_copy(
                Gkk[:, :], psC[:, 512:640]), deps=[gk_tok])
            vtr_all = [vtr_t[-1]]

            DQ2, DK2, NQ, NK, RQ, RK, DQ, RM, RS, RDEN = \
                0, 1, 2, 3, 4, 5, 6, 7, 9, 10

            def sv(i):
                return smallv[:, i:i + 1]

            t = pl.add("V", lambda e: e.tensor_tensor(
                sim1[:, :], Gq[:, 0:P], identf[:, :], ALU.mult),
                deps=[gq_ev, identf_t])
            t = pl.add("V", lambda e: e.reduce_sum(sv(DQ2), sim1[:, :], AX),
                       deps=[t])
            t = pl.add("S", lambda e: e.activation(sv(NQ), sv(DQ2), ACT.Sqrt),
                       deps=[t])
            t = pl.add("V", lambda e: e.tensor_scalar(
                sv(NQ), sv(NQ), L2_EPS, None, ALU.max), deps=[t])
            t = pl.add("V", lambda e: e.reciprocal(sv(RQ), sv(NQ)), deps=[t])
            dq_tok = pl.add("V", lambda e: e.tensor_tensor(
                sv(DQ), sv(RQ), smallv[:, ET:ET + 1], ALU.mult),
                deps=[t, et_col])

            t = pl.add("V", lambda e: e.tensor_tensor(
                sim2[:, :], Gkk[:, :], identf[:, :], ALU.mult),
                deps=[gk_ev, identf_t])
            t = pl.add("V", lambda e: e.reduce_sum(sv(DK2), sim2[:, :], AX),
                       deps=[t])
            t = pl.add("S", lambda e: e.activation(sv(NK), sv(DK2), ACT.Sqrt),
                       deps=[t])
            t = pl.add("V", lambda e: e.tensor_scalar(
                sv(NK), sv(NK), L2_EPS, None, ALU.max), deps=[t])
            rk_tok = pl.add("V", lambda e: e.reciprocal(sv(RK), sv(NK)),
                            deps=[t])
            t = pl.add("T", lambda e: e.transpose(
                ps_tf[0:1, 0:P], smallv[:, RK:RK + 1], identf[:, :]),
                deps=[rk_tok, woutT_done])
            t = pl.add("V", lambda e: e.tensor_copy(
                dkb[0:1, 0:P], ps_tf[0:1, 0:P]), deps=[t])
            t = pl.add("T", lambda e: e.matmul(
                psC[:, 0:P], onesh[0:1, :], dkb[0:1, 0:P],
                start=True, stop=True), deps=[t, onesf_t, gq_ev])
            dkb_tok = pl.add("V", lambda e: e.tensor_copy(
                dkb[:, :], psC[:, 0:P]), deps=[t])

            t = pl.add("V", lambda e: e.tensor_scalar(
                sim1[:, :], Gq[:, P:256], sv(DQ), None, ALU.mult),
                deps=[gq_ev, dq_tok])
            t = pl.add("V", lambda e: e.tensor_tensor(
                sim2[:, :], sim1[:, :], dkb[:, :], ALU.mult),
                deps=[t, dkb_tok])
            t = pl.add("V", lambda e: e.tensor_tensor(
                sim2[:, :], sim2[:, :], maskb[:, :], ALU.add),
                deps=[t, mask_t])
            t_rm = pl.add("V", lambda e: e.reduce_max(sv(RM), sim2[:, :], AX),
                          deps=[t])
            t_rm = pl.add("V", lambda e: e.tensor_scalar(
                sv(RM), sv(RM), -1.0, None, ALU.mult), deps=[t_rm])
            t = pl.add("S", lambda e: e.activation(
                sim1[:, :], sim2[:, :], ACT.Exp, bias=sv(RM), scale=1.0,
                accum_out=sv(RS)), deps=[t_rm])
            t = pl.add("V", lambda e: e.reciprocal(sv(RDEN), sv(RS)), deps=[t])
            p_tok = pl.add("V", lambda e: e.tensor_scalar(
                Pbf[:, :], sim1[:, :], sv(RDEN), None, ALU.mult), deps=[t])
            t = pl.add("T", lambda e: e.transpose(
                ps_tb[:, 0:P], Pbf[:, :], identb[:, :]),
                deps=[p_tok, wfft_done])
            pt_tok = pl.add("V", lambda e: e.tensor_copy(
                PTb[:, :], ps_tb[:, 0:P]), deps=[t])

            av_banks = [psC[:, 0:512], psC[:, 512:1024]]
            av_ev = []
            bank_free = [dkb_tok, gk_ev]
            for g in range(4):
                bk = g % 2
                tok = None
                for j in range(4):
                    tc = 4 * g + j
                    tok = pl.add("T", (lambda tc, bk, j: lambda e: e.matmul(
                        av_banks[bk][:, j * P:(j + 1) * P], PTb[:, :],
                        arena[:, A_VF + tc * P: A_VF + (tc + 1) * P],
                        start=(j == 0), stop=(j == 3)))(tc, bk, j),
                        deps=(([pt_tok, bank_free[bk]] + vtr_all)
                              if j == 0 else []),
                        inc=(1 if j == 3 else 0))
                ev = pl.add("S", (lambda g, bk: lambda e: e.activation(
                    attn_fm[:, g * 512:(g + 1) * 512], av_banks[bk],
                    ACT.Identity))(g, bk), deps=[tok])
                av_ev.append(ev)
                bank_free[bk] = ev

            xplus = [None] * NT
            op_slots = [psA[:, :], psC[:, :]]
            slot_free = [None, Tok("S", pl.counts["S"])]
            for k in range(NT):
                sl = k % 2
                dd = [av_ev[k // 4], woutT_done, slot_free[sl]]
                pl.add("T", (lambda k, sl: lambda e: e.matmul(
                    op_slots[sl][:, 0:512], attn_fm[:, k * P:(k + 1) * P],
                    WoutT[:, 0:512], start=True, stop=True))(k, sl),
                    deps=dd, inc=0)
                t2 = pl.add("T", (lambda k, sl: lambda e: e.matmul(
                    op_slots[sl][:, 512:1024], attn_fm[:, k * P:(k + 1) * P],
                    WoutT[:, 512:1024], start=True, stop=True))(k, sl),
                    deps=[], inc=1)
                xplus[k] = pl.add("V", (lambda k, sl: lambda e:
                    e.scalar_tensor_tensor(
                        X[:, k * D:(k + 1) * D], op_slots[sl],
                        0.0, X[:, k * D:(k + 1) * D], ALU.add, ALU.add,
                        accum_out=col(SUMS, k)))(k, sl),
                    deps=[t2, sq_t[k], xn_t[k]])
                slot_free[sl] = xplus[k]
            return dict(
                xplus=xplus,
                qdT_reader=Tok("T", pl.counts["T"], "T"),
                arena_reader=Tok("T", pl.counts["T"], "T"),
                vscr_user=vscr_user,
                sq_t=sq_t,
                tra_last=[tra_t[12], tra_t[13], tra_t[14], tra_t[15]],
            )

        final_x = xin
        if n_sub >= 1:
            xs_reader1 = [None] * NT
            r1 = ffn(1, xin, xs_reader1, None, woutT_done,
                     Tok("T", 0, "T"), wfft_done,
                     (None, None, None), None,
                     [None, None])
            final_x = r1["xplus"]
        if n_sub >= 2:
            xs_reader2 = [None] * NT
            r2 = attn(r1["xplus"], xs_reader2, r1["qdT_reader"],
                      r1["arena_reader"], r1["vscr_user"],
                      [r1["qdr_free"][0], r1["qdr_free"][1],
                       r1["qdr_free"][2], r1["qdr_free"][3]])
            final_x = r2["xplus"]
        if n_sub >= 3:
            xs_reader3 = [None] * NT
            r3 = ffn(2, r2["xplus"], xs_reader3, r2["xplus"], r1["etm_reader"],
                     r2["qdT_reader"], r2["arena_reader"], r2["vscr_user"],
                     r1["q8_free"],
                     r2["tra_last"])
            final_x = r3["xplus"]

        out_toks = []
        for k in range(NT):
            out_toks.append(pl.add("Y", (lambda k: lambda e: e.dma_start(
                out_d[k * P:(k + 1) * P, :], X[:, k * D:(k + 1) * D]))(k),
                deps=[final_x[k], woutT_done, wfft_done, et_col, wsh_t, bh_t],
                inc=DMA, sem="OUT"))

        if debug:
            for nm, tens, width in (("stats", stats, 15 * NT),
                                    ("q8r", q8r, 2 * D), ("qdr", qdr, 2 * D),
                                    ("qdTa", qdTa, KC * 1024),
                                    ("arena", arena, NOB * T),
                                    ("e_tm", e_tm, NT * D),
                                    ("WffT", WffT, KC * D),
                                    ("cols", cols, 40)):
                dt = tens[:].tensor.dtype
                dbg_d[nm] = nc.dram_tensor("dbg_" + nm, [P, width], dt,
                                           kind="ExternalOutput").ap()
                deps = [final_x[NT - 1],
                        Tok("V", pl.counts["V"]), Tok("S", pl.counts["S"]),
                        Tok("G", pl.counts["G"]), Tok("T", pl.counts["T"])]
                deps += [Tok(sm, pl.counts[sm]) for sm in DMA_SEMS
                         if sm != "OUT" and pl.counts[sm]]
                pl.add("Y", (lambda nm, tens, width: lambda e: e.dma_start(
                    dbg_d[nm], tens[:, 0:width]))(nm, tens, width),
                    deps=deps, inc=DMA, sem="OUT")

        final_out = pl.counts["OUT"]

        @block.sync
        def _(sync):
            pl.emit("Y", sync, sems)
            sync.wait_ge(sems["OUT"], final_out)

        @block.tensor
        def _(tensor):
            pl.emit("T", tensor, sems)

        @block.vector
        def _(vector):
            pl.emit("V", vector, sems)

        @block.scalar
        def _(scalar):
            pl.emit("S", scalar, sems)

        @block.gpsimd
        def _(gpsimd):
            pl.emit("G", gpsimd, sems)

    return nc


def kernel(L0_attn_x, L0_attn_norm_weight, L0_attn_w_qkv, L0_attn_temperature,
           L0_attn_w_out, L0_ffn_weight_q, L0_ffn_weight_scale, L0_ffn_bias):
    x = np.ascontiguousarray(np.asarray(L0_attn_x, dtype=np.float32))
    wq = np.ascontiguousarray(np.asarray(L0_ffn_weight_q, dtype=np.int8))
    ws = np.ascontiguousarray(np.asarray(L0_ffn_weight_scale, dtype=np.float32))
    bias = np.ascontiguousarray(np.asarray(L0_ffn_bias, dtype=np.float32))
    wqkv = np.ascontiguousarray(np.asarray(L0_attn_w_qkv, dtype=np.float32))
    wout = np.ascontiguousarray(np.asarray(L0_attn_w_out, dtype=np.float32))
    nw = np.ascontiguousarray(np.asarray(L0_attn_norm_weight, dtype=np.float32))
    temp = np.ascontiguousarray(
        np.asarray(L0_attn_temperature, dtype=np.float32).reshape(8))

    ident = np.eye(P, dtype=np.float32)
    hb = (np.arange(P) // 16)
    mask = np.where(hb[:, None] == hb[None, :], 0.0,
                    -60000.0).astype(np.float16)
    auxf = np.ascontiguousarray(ident)
    auxb = np.ascontiguousarray(
        np.concatenate([ident.astype(np.float16), mask], axis=1))

    wqT = np.ascontiguousarray(
        wq.T.reshape(8, 128, 1024).transpose(1, 0, 2).reshape(128, 8192))
    cols_in = np.ascontiguousarray(np.concatenate(
        [ws.reshape(8, 128).T, bias.reshape(8, 128).T,
         nw.reshape(8, 128).T], axis=1))
    wqkvT = np.ascontiguousarray(
        wqkv.T.reshape(8, 128, 384).transpose(1, 0, 2).reshape(128, 3072))
    woutT = np.ascontiguousarray(wout.T)

    nc = build_program()
    in_maps = []
    for b in range(8):
        in_maps.append({
            "x": np.ascontiguousarray(x[b]),
            "wqT": wqT, "cols_in": cols_in,
            "wqkvT": wqkvT, "woutT": woutT, "temp": temp,
            "auxf": auxf, "auxb": auxb,
        })
    res = run_bass_kernel_spmd(nc, in_maps, core_ids=list(range(8)))
    out = np.stack([np.asarray(r["out"], dtype=np.float32)
                    for r in res.results], axis=0)
    return out


# version 31 (final)
# speedup vs baseline: 1.2127x; 1.2127x over previous
import numpy as np
import ml_dtypes
from contextlib import ExitStack

import concourse.bass as bass
import concourse.mybir as mybir
from concourse.bass_utils import run_bass_kernel_spmd

F32 = mybir.dt.float32
BF16 = mybir.dt.bfloat16
F16 = mybir.dt.float16
I8 = mybir.dt.int8
ALU = mybir.AluOpType
ACT = mybir.ActivationFunctionType
AX = mybir.AxisListType.X

P = 128
NT = 16
D = 1024
KC = 8
NOB = 8
T = NT * P
MAGIC = 12582912.0
LN_EPS = 1e-5
RMS_EPS = 1e-6
Q_EPS = 1e-6
L2_EPS = 1e-12

ENGINES = ("Y", "T", "V", "S", "G")
DMA_SEMS = tuple("XIN%d" % i for i in range(16)) + (
    "WQ", "WKV", "SM", "TRA0", "TRA1", "TRA2", "TRA3", "TRB0", "TRB1",
    "TRB2", "TRB3", "VTR", "OUT")
ALL_SEMS = ENGINES[1:] + DMA_SEMS


class Tok:
    __slots__ = ("sem", "cnt", "stream")

    def __init__(self, sem, cnt, stream=None):
        self.sem = sem
        self.cnt = cnt
        self.stream = stream


class Planner:
    def __init__(self):
        self.items = {e: [] for e in ENGINES}
        self.counts = {s: 0 for s in ALL_SEMS}
        self._clock = {e: {s: 0 for s in ALL_SEMS} for e in ENGINES}
        self._snap = {s: {0: {t: 0 for t in ALL_SEMS}} for s in ALL_SEMS}
        self.n_waits = 0

    def add(self, eng, builder, deps=(), inc=1, sem=None):
        deps = [d for d in deps if d is not None]
        if sem is None:
            sem = eng
        cl = self._clock[eng]
        waits = []
        for d in deps:
            assert isinstance(d, Tok), d
            assert d.cnt <= self.counts[d.sem], "dep on future token"
            if cl.get(d.sem, 0) >= d.cnt:
                continue
            waits.append((d.sem, d.cnt))
            snap = self._best_snap(d.sem, d.cnt)
            for f in ALL_SEMS:
                if snap[f] > cl[f]:
                    cl[f] = snap[f]
            cl[d.sem] = max(cl[d.sem], d.cnt)
        self.n_waits += len(waits)
        self.items[eng].append((waits, builder, sem if inc else None, inc))
        if inc:
            self.counts[sem] += inc
            self._snap[sem][self.counts[sem]] = dict(cl)
        return Tok(sem, self.counts[sem], eng)

    def _best_snap(self, sem, cnt):
        snaps = self._snap[sem]
        if cnt in snaps:
            return snaps[cnt]
        best = 0
        for c in snaps:
            if best < c <= cnt:
                best = c
        return snaps[best]

    def emit(self, eng_name, eng_obj, sems):
        for waits, builder, inc_sem, inc in self.items[eng_name]:
            for (ws, wc) in waits:
                eng_obj.wait_ge(sems[ws], wc)
            inst = builder(eng_obj)
            if inc:
                inst.then_inc(sems[inc_sem], inc)


def build_program(n_sub=3, parts=('vec', 'et', 'wff', 'wqkv', 'wout'), debug=False):
    nc = bass.Bass("TRN2", target_bir_lowering=False,
                   detect_race_conditions=False)

    x_in = nc.dram_tensor("x", [T, D], F32, kind="ExternalInput").ap()
    wqT_in = nc.dram_tensor("wqT", [P, KC * D], I8,
                            kind="ExternalInput").ap()
    cols_in = nc.dram_tensor("cols_in", [P, 24], F32, kind="ExternalInput").ap()
    wqkvT_in = nc.dram_tensor("wqkvT", [P, KC * 384], F32, kind="ExternalInput").ap()
    woutT_in = nc.dram_tensor("woutT", [P, D], F32, kind="ExternalInput").ap()
    temp_in = nc.dram_tensor("temp", [8], F32, kind="ExternalInput").ap()
    auxf_in = nc.dram_tensor("auxf", [P, P], F32, kind="ExternalInput").ap()
    auxb_in = nc.dram_tensor("auxb", [P, 2 * P], F16, kind="ExternalInput").ap()
    out_d = nc.dram_tensor("out", [T, D], F32, kind="ExternalOutput").ap()
    dbg_d = {}

    with ExitStack() as st:
        def sb(name, shape, dtype):
            return st.enter_context(nc.sbuf_tensor(name, shape, dtype))

        def psalloc(name, shape, dtype=F32):
            return st.enter_context(nc.psum_tensor(name, shape, dtype))

        X = sb("X", [P, NT * D], F32)
        WffT = sb("WffT", [P, KC * D], F16)
        WqkvT = sb("WqkvT", [P, KC * 384], F16)
        WoutT = sb("WoutT", [P, D], F16)
        arena = sb("arena", [P, NOB * T], F16)
        e_tm = sb("e_tm", [P, NT * D], F16)
        qdTa = sb("qdTa", [P, KC * 1536], F16)
        attn_fm = sb("attn_fm", [P, T], F16)
        v_scr = sb("v_scr", [P, 3 * D], F32)
        q8r = sb("q8r", [P, 2 * D], I8)
        qdr = sb("qdr", [P, 4 * D], F16)
        xnr = qdr
        stats = sb("stats", [P, 15 * NT], F32)
        Gq = sb("Gq", [P, 256], F32)
        Gkk = sb("Gkk", [P, P], F32)
        sim1 = sb("sim1", [P, P], F32)
        sim2 = sb("sim2", [P, P], F32)
        Pbf = sb("Pbf", [P, P], F16)
        PTb = sb("PTb", [P, P], F16)
        dkb = sb("dkb", [P, P], F16)
        maskb = sb("maskb", [P, P], F16)
        identb = sb("identb", [P, P], F16)
        identf = sb("identf", [P, P], F32)
        onesh = sb("onesh", [1, P], F16)
        cols = sb("cols", [P, 40], F32)
        smallv = sb("smallv", [P, 16], F32)

        psA = psalloc("psA", [P, 1024], F32)
        psB = psalloc("psB", [P, 1024], F32)
        psC = psalloc("psC", [P, 1024], F32)
        ps_tb = psalloc("ps_tb", [P, 512], F16)
        ps_tf = psalloc("ps_tf", [P, 512], F32)

        sems = {e: st.enter_context(nc.semaphore("sem_" + e))
                for e in ALL_SEMS}
        block = st.enter_context(nc.Block())

        pl = Planner()

        qdTa3 = qdTa[:].rearrange("c (kc t) -> c kc t", kc=KC)
        etm3 = e_tm[:].rearrange("t (tc o) -> t tc o", tc=NT)
        A_QK = 0
        A_V = NT * 256
        A_VF = A_V + NT * P

        SUMS, SUMSQ, MEAN, SCOL, AMAX, INVXS, XSC, TMP1, TMP2, TMP3 = \
            (i * NT for i in range(10))
        CDOL = 10 * NT
        ETCOL = 11 * NT
        MAXC, MINC = AMAX, INVXS
        ACOL, BCOL, EPSC = 12 * NT, 13 * NT, 14 * NT
        C_WS, C_BI, C_NW, C_WSH, C_BH = 0, 8, 16, 24, 32
        DMA = 16

        def col(base, k):
            return stats[:, base + k: base + k + 1]

        def slab_sl(base, s):
            return stats[:, base + 4 * s: base + 4 * (s + 1)]

        temp_ld = pl.add("S", lambda e: e.dma_start(
            stats[0:1, ETCOL:ETCOL + 8],
            temp_in[:].rearrange("(a t) -> a t", a=1)),
            inc=DMA, sem="SM")
        identf_t = pl.add("S", lambda e: e.dma_start(
            identf[:, :], auxf_in[:, 0:P]), inc=DMA, sem="SM")
        mask_t = pl.add("S", lambda e: e.dma_start(
            maskb[:, :], auxb_in[:, P:2 * P]), inc=DMA, sem="SM")
        identb_t = pl.add("S", lambda e: e.dma_start(
            identb[:, :], auxb_in[:, 0:P]), inc=DMA, sem="SM")
        cols_ld = pl.add("S", lambda e: e.dma_start(
            cols[:, C_WS:C_WS + 24], cols_in[:, :]), inc=DMA, sem="SM")
        stg_i8 = e_tm[:, 12288:16384].bitcast(I8)
        stg_f32 = e_tm[:, 6144:12288].bitcast(F32)
        stg_wo = e_tm[:, 4096:6144].bitcast(F32)
        wout_ld = pl.add("S", lambda e: e.dma_start(
            stg_wo[:, 0:1024], woutT_in[:, :]), inc=DMA, sem="SM")
        sm_all = wout_ld
        temp_ld = identf_t = mask_t = identb_t = cols_ld = sm_all
        ws_ld = bi_ld = nw_ld = sm_all

        wq_lds = pl.add("S", lambda e: e.dma_start(
            stg_i8[:, :], wqT_in[:, :]), inc=DMA, sem="WQ")
        wff_cvs = []
        for h in range(2):
            cv = pl.add("V", (lambda h: lambda e: e.tensor_copy(
                WffT[:, h * 4 * D:(h + 1) * 4 * D],
                stg_i8[:, h * 4 * D:(h + 1) * 4 * D]))(h),
                deps=([wq_lds] if h == 0 else []))
            wff_cvs.append(cv)
        wfft_done = wff_cvs[-1]

        wkv_lds = pl.add("S", lambda e: e.dma_start(
            stg_f32[:, :], wqkvT_in[:, :]), inc=DMA, sem="WKV")
        wqkv_cvs = []
        for kc in range(KC):
            cv = pl.add("V", (lambda kc: lambda e: e.tensor_scalar(
                WqkvT[:, kc * 384:(kc + 1) * 384],
                stg_f32[:, kc * 384:(kc + 1) * 384],
                cols[:, C_NW + kc:C_NW + kc + 1], None,
                ALU.mult))(kc),
                deps=([wkv_lds, nw_ld] if kc == 0 else []))
            wqkv_cvs.append(cv)
        wqkvT_done = wqkv_cvs[-1]
        woutT_done = pl.add("V", lambda e: e.tensor_copy(
            WoutT[:, :], stg_wo[:, 0:1024]), deps=[wout_ld])

        xin = []
        for k in range(NT):
            xin.append(pl.add("Y", (lambda k: lambda e: e.dma_start(
                X[:, k * D:(k + 1) * D], x_in[k * P:(k + 1) * P, :]))(k),
                deps=([wq_lds] if k == 4 else []),
                inc=DMA, sem="XIN%d" % k))

        epsc_t = pl.add("V", lambda e: e.memset(
            stats[:, EPSC:EPSC + NT], Q_EPS))
        onesf_t = pl.add("V", lambda e: e.memset(onesh[:, :], 1.0))
        wsh_t = pl.add("V", lambda e: e.tensor_scalar(
            cols[:, C_WSH:C_WSH + 8], cols[:, C_WS:C_WS + 8], 0.5, None,
            ALU.mult), deps=[ws_ld])
        bh_t = pl.add("V", lambda e: e.tensor_scalar(
            cols[:, C_BH:C_BH + 8], cols[:, C_BI:C_BI + 8], 0.5, None,
            ALU.mult), deps=[bi_ld])

        ET = 8
        et_exp = pl.add("S", lambda e: e.activation(
            stats[0:1, ETCOL + 8:ETCOL + 16], stats[0:1, ETCOL:ETCOL + 8],
            ACT.Exp), deps=[temp_ld])
        et_bf = pl.add("V", lambda e: e.tensor_copy(
            Pbf[0:1, 0:8], stats[0:1, ETCOL + 8:ETCOL + 16]),
            deps=[et_exp])
        et_mm = pl.add("T", lambda e: e.matmul(
            psC[:, 0:8], onesh[0:1, :], Pbf[0:1, 0:8], start=True,
            stop=True), deps=[et_bf, onesf_t])
        et_ev = pl.add("V", lambda e: e.tensor_copy(
            stats[:, ETCOL:ETCOL + 8], psC[:, 0:8]), deps=[et_mm])
        et_s1 = pl.add("G", lambda e: e.affine_select(
            stats[:, ETCOL + 8:ETCOL + 16], stats[:, ETCOL:ETCOL + 8],
            [[-16, 8]], ALU.is_ge, 0.0, base=0, channel_multiplier=1),
            deps=[et_ev])
        et_s2 = pl.add("G", lambda e: e.affine_select(
            stats[:, ETCOL:ETCOL + 8], stats[:, ETCOL + 8:ETCOL + 16],
            [[16, 8]], ALU.is_ge, 0.0, base=15, channel_multiplier=-1),
            deps=[et_s1])
        et_col = pl.add("V", lambda e: e.reduce_sum(
            smallv[:, ET:ET + 1], stats[:, ETCOL:ETCOL + 8], AX),
            deps=[et_s2])

        def ffn(sub, x_ready, x_s_reader, sums_ready, prev_etm_reader,
                prev_qdT_reader, prev_arena_reader, init_vscr, init_q8,
                init_qdr):
            sq_t = [None] * NT
            sum_t = [None] * NT
            y_t = [None] * NT
            am_t = [None] * NT
            tra_t = [None] * NT
            vscr_user = list(init_vscr)
            q8_free = [init_q8, init_q8]
            qdr_free = list(init_qdr) + [None] * (4 - len(init_qdr))
            fm_last = [None] * 4
            evac_t = {}
            if sub == 1:
                fm_slots = [psB[:, 0:512], psB[:, 512:1024],
                            psA[:, 0:512], psA[:, 512:1024]]
            else:
                fm_slots = [psB[:, 0:512], psB[:, 512:1024]]
            n_slots = len(fm_slots)
            slot_free = [prev_arena_reader] * n_slots

            def phase1_emit(g):
                for k in range(4 * g, 4 * g + 4):
                    sl = 2
                    deps_sq = [x_ready[k], vscr_user[2]]
                    if sub == 1:
                        sum_t[k] = pl.add("S", (lambda k: lambda e:
                            e.activation(
                                v_scr[:, 2 * D:3 * D], X[:, k * D:(k + 1) * D],
                                ACT.Identity, accum_out=col(SUMS, k)))(k),
                            deps=[x_ready[k], vscr_user[2]])
                        deps_sq.append(sum_t[k])
                    if sums_ready is not None:
                        deps_sq.append(sums_ready[k])
                    sq_t[k] = pl.add("S", (lambda k, sl: lambda e: e.activation(
                        v_scr[:, sl * D:(sl + 1) * D], X[:, k * D:(k + 1) * D],
                        ACT.Square, accum_out=col(SUMSQ, k)))(k, sl),
                        deps=deps_sq)
                    vscr_user[2] = sq_t[k]
                    x_s_reader[k] = sq_t[k]
                    am_t[k] = pl.add("V", (lambda k: lambda e: e.reduce_max(
                        col(MAXC, k), X[:, k * D:(k + 1) * D], AX))(k),
                        deps=[x_ready[k]])
                    y_t[k] = pl.add("V", (lambda k: lambda e: e.tensor_reduce(
                        col(MINC, k), X[:, k * D:(k + 1) * D], AX,
                        ALU.min))(k), deps=[x_ready[k]])

            math2_g = [None] * 4

            def phase2_emit(g):
                def R(base, g=g):
                    return stats[:, base + 4 * g: base + 4 * g + 4]

                ks = range(4 * g, 4 * g + 4)
                d1 = [sq_t[k] for k in ks] + [am_t[k] for k in ks] + \
                    [y_t[k] for k in ks] + \
                    ([sum_t[k] for k in ks] if sub == 1 else [])
                t = pl.add("V", lambda e, R=R: e.tensor_scalar(
                    R(MEAN), R(SUMS), 1.0 / D, None, ALU.mult), deps=d1)
                t = pl.add("V", lambda e, R=R: e.tensor_scalar(
                    R(TMP1), R(SUMSQ), 1.0 / D, None, ALU.mult), deps=[t])
                t = pl.add("S", lambda e, R=R: e.activation(
                    R(TMP2), R(MEAN), ACT.Square), deps=[t])
                t = pl.add("V", lambda e, R=R: e.tensor_tensor(
                    R(TMP1), R(TMP1), R(TMP2), ALU.subtract), deps=[t])
                t = pl.add("V", lambda e, R=R: e.tensor_scalar(
                    R(TMP1), R(TMP1), LN_EPS, None, ALU.add), deps=[t])
                t = pl.add("V", lambda e, R=R: e.reciprocal(R(TMP2), R(TMP1)),
                           deps=[t])
                t = pl.add("S", lambda e, R=R: e.activation(
                    R(SCOL), R(TMP2), ACT.Sqrt), deps=[t])
                t = pl.add("V", lambda e, R=R: e.tensor_tensor(
                    R(TMP1), R(MAXC), R(MEAN), ALU.subtract), deps=[t])
                t = pl.add("V", lambda e, R=R: e.tensor_tensor(
                    R(TMP2), R(MEAN), R(MINC), ALU.subtract), deps=[t])
                t = pl.add("V", lambda e, R=R: e.tensor_tensor(
                    R(TMP1), R(TMP1), R(TMP2), ALU.max), deps=[t])
                t = pl.add("V", lambda e, R=R: e.tensor_tensor(
                    R(TMP1), R(TMP1), R(SCOL), ALU.mult), deps=[t])
                t = pl.add("V", lambda e, R=R: e.tensor_tensor(
                    R(TMP1), R(TMP1), R(EPSC), ALU.max), deps=[t, epsc_t])
                t = pl.add("V", lambda e, R=R: e.tensor_scalar(
                    R(XSC), R(TMP1), 1.0 / 127.0, None, ALU.mult), deps=[t])
                t = pl.add("V", lambda e, R=R: e.reciprocal(R(TMP2), R(TMP1)),
                           deps=[t])
                t = pl.add("V", lambda e, R=R: e.tensor_scalar(
                    R(TMP2), R(TMP2), 127.0, None, ALU.mult), deps=[t])
                t = pl.add("V", lambda e, R=R: e.tensor_tensor(
                    R(ACOL), R(SCOL), R(TMP2), ALU.mult), deps=[t])
                t = pl.add("V", lambda e, R=R: e.tensor_tensor(
                    R(BCOL), R(MEAN), R(ACOL), ALU.mult), deps=[t])
                math2_g[g] = pl.add("V", lambda e, R=R: e.tensor_scalar(
                    R(BCOL), R(BCOL), -1.0, None, ALU.mult), deps=[t])

            def quant_emit(s):
                for j in range(4):
                    k = 4 * s + j
                    sl = k % 2
                    sq8 = k % 2
                    t = pl.add("V", (lambda k, sl: lambda e: e.tensor_scalar(
                        v_scr[:, sl * D:(sl + 1) * D], X[:, k * D:(k + 1) * D],
                        col(MEAN, k), col(ACOL, k), ALU.subtract,
                        ALU.mult))(k, sl),
                        deps=[math2_g[s], vscr_user[sl], sq_t[k]])
                    t = pl.add("V", (lambda k, sl, sq8: lambda e: e.tensor_scalar(
                        q8r[:, sq8 * D:(sq8 + 1) * D],
                        v_scr[:, sl * D:(sl + 1) * D],
                        MAGIC, MAGIC, ALU.add, ALU.subtract))(k, sl, sq8),
                        deps=[t, q8_free[sq8]])
                    vscr_user[sl] = t
                    deq = pl.add("S", (lambda k, sq8: lambda e: e.activation(
                        qdr[:, (k % 4) * D:(k % 4 + 1) * D],
                        q8r[:, sq8 * D:(sq8 + 1) * D],
                        ACT.Identity, scale=col(XSC, k)))(k, sq8),
                        deps=[t, qdr_free[k % 4]])
                    q8_free[sq8] = deq
                    slot_dep = prev_qdT_reader if k < 12 else \
                        fm_last[(k - 12) // 4]
                    kslot = (k // 4) % 3 * 4 + k % 4
                    tra_t[k] = pl.add("Y", (lambda k, kslot: lambda e:
                        e.dma_start(
                            qdTa3[:, :, kslot * P:(kslot + 1) * P],
                            qdr[:, (k % 4) * D:(k % 4 + 1) * D],
                            transpose=True))(k, kslot),
                        deps=[deq, slot_dep], inc=DMA,
                        sem="TRA%d" % (k % 4))
                    qdr_free[k % 4] = tra_t[k]

            def fm_emit(s):
                slab_deps = [tra_t[4 * s + j] for j in range(4)] + [wfft_done]
                for ob in range(NOB):
                    slot = (s * NOB + ob) % n_slots
                    dd = slab_deps + [slot_free[slot]]
                    tok = None
                    for kc in range(KC):
                        tok = pl.add("T", (lambda s, ob, kc, slot: lambda e:
                            e.matmul(
                                fm_slots[slot],
                                WffT[:, kc * D + ob * P: kc * D + (ob + 1) * P],
                                qdTa3[:, kc,
                                      (s % 3) * 512:(s % 3) * 512 + 512],
                                start=(kc == 0),
                                stop=(kc == KC - 1)))(s, ob, kc, slot),
                            deps=(dd if kc == 0 else []),
                            inc=(1 if kc == KC - 1 else 0))
                    fm_last[s] = tok
                    ev = pl.add("S", (lambda s, ob, slot: lambda e: e.activation(
                        arena[:, ob * T + s * 512: ob * T + (s + 1) * 512],
                        fm_slots[slot], ACT.Identity,
                        scale=cols[:, C_WSH + ob:C_WSH + ob + 1],
                        bias=cols[:, C_BH + ob:C_BH + ob + 1]))(s, ob, slot),
                        deps=[tok, wsh_t, bh_t, prev_arena_reader])
                    evac_t[(s, ob)] = ev
                    slot_free[slot] = ev

            for g in range(4):
                phase1_emit(g)
            for g in range(4):
                phase2_emit(g)
            trb_t = {}

            def trb_emit(q):
                for ob in range(NOB):
                    dd = [evac_t[(q, ob)], prev_etm_reader]
                    trb_t[(q, ob)] = pl.add("Y", (lambda q, ob: lambda e:
                        e.dma_start(
                            etm3[:, q * 4:(q + 1) * 4, ob * P:(ob + 1) * P],
                            arena[:, ob * T + q * 512: ob * T + (q + 1) * 512],
                            transpose=True))(q, ob), inc=DMA, deps=dd,
                        sem="TRB%d" % q)

            quant_emit(0)
            for g in range(4):
                if g < 3:
                    quant_emit(g + 1)
                fm_emit(g)
                trb_emit(g)

            xplus = [None] * NT
            trb_last = {q: max((trb_t[(q, ob)] for ob in range(NOB)),
                               key=lambda t: t.cnt) for q in range(4)}
            for k in range(NT):
                h = k // 4
                dd = [trb_last[h]]
                dd += [sq_t[k]]
                if sub == 1:
                    xplus[k] = pl.add("V", (lambda k: lambda e:
                        e.scalar_tensor_tensor(
                            X[:, k * D:(k + 1) * D], e_tm[:, k * D:(k + 1) * D],
                            0.0, X[:, k * D:(k + 1) * D], ALU.add, ALU.add,
                            accum_out=col(SUMS, k)))(k), deps=dd)
                else:
                    xplus[k] = pl.add("V", (lambda k: lambda e:
                        e.scalar_tensor_tensor(
                            X[:, k * D:(k + 1) * D], e_tm[:, k * D:(k + 1) * D],
                            0.0, X[:, k * D:(k + 1) * D], ALU.add,
                            ALU.add))(k), deps=dd)
            return dict(
                xplus=xplus,
                etm_reader=xplus[-1],
                qdT_reader=Tok("T", pl.counts["T"], "T"),
                arena_reader=trb_t[(1, NOB - 1)],
                vscr_user=vscr_user,
                q8_free=q8_free[0 if q8_free[0].cnt > q8_free[1].cnt else 1],
                qdr_free=qdr_free,
                tra_last=tra_t[-1],
            )

        def attn(x_ready, x_s_reader, prev_qdT_reader, prev_arena_reader,
                 init_vscr, init_xnr):
            sq_t = [None] * NT
            xn_t = [None] * NT
            tra_t = [None] * NT
            qkv_mm = [None] * NT
            qkv_ev = [None] * NT
            xnr_free = list(init_xnr)
            vscr_user = list(init_vscr)
            qkv_slots = [psB[:, 0:384], psB[:, 512:512 + 384],
                         psA[:, 0:384], psA[:, 512:512 + 384]]
            vtr_t = []

            math_c_g = [None] * 4

            def aph1_emit(g):
                for k in range(4 * g, 4 * g + 4):
                    sl = 2
                    sq_t[k] = pl.add("S", (lambda k, sl: lambda e: e.activation(
                        v_scr[:, sl * D:(sl + 1) * D], X[:, k * D:(k + 1) * D],
                        ACT.Square, accum_out=col(SUMSQ, k)))(k, sl),
                        deps=[x_ready[k], vscr_user[2]])
                    vscr_user[2] = sq_t[k]
                    x_s_reader[k] = sq_t[k]

            def amath_emit(g):
                def R(base, g=g):
                    return stats[:, base + 4 * g: base + 4 * g + 4]

                ks = range(4 * g, 4 * g + 4)
                d1 = [sq_t[k] for k in ks] + [x_ready[k] for k in ks]
                t = pl.add("V", lambda e, R=R: e.tensor_scalar(
                    R(MEAN), R(SUMS), 1.0 / D, None, ALU.mult), deps=d1)
                t = pl.add("V", lambda e, R=R: e.tensor_scalar(
                    R(TMP1), R(SUMSQ), 1.0 / D, None, ALU.mult), deps=[t])
                t = pl.add("S", lambda e, R=R: e.activation(
                    R(TMP2), R(MEAN), ACT.Square), deps=[t])
                t = pl.add("V", lambda e, R=R: e.tensor_tensor(
                    R(TMP1), R(TMP1), R(TMP2), ALU.subtract), deps=[t])
                t = pl.add("V", lambda e, R=R: e.tensor_scalar(
                    R(TMP2), R(TMP1), LN_EPS, None, ALU.add), deps=[t])
                t = pl.add("V", lambda e, R=R: e.reciprocal(R(TMP3), R(TMP2)),
                           deps=[t])
                t_s = pl.add("S", lambda e, R=R: e.activation(
                    R(SCOL), R(TMP3), ACT.Sqrt), deps=[t])
                t = pl.add("V", lambda e, R=R: e.tensor_tensor(
                    R(TMP2), R(TMP1), R(TMP3), ALU.mult), deps=[t_s])
                t = pl.add("V", lambda e, R=R: e.tensor_scalar(
                    R(TMP2), R(TMP2), RMS_EPS, None, ALU.add), deps=[t])
                t = pl.add("V", lambda e, R=R: e.reciprocal(R(TMP1), R(TMP2)),
                           deps=[t])
                t = pl.add("S", lambda e, R=R: e.activation(
                    R(TMP2), R(TMP1), ACT.Sqrt), deps=[t])
                math_c_g[g] = pl.add("V", lambda e, R=R: e.tensor_tensor(
                    R(CDOL), R(SCOL), R(TMP2), ALU.mult), deps=[t])

            def aqkv_emit(g):
                for k in range(4 * g, 4 * g + 4):
                    xn_t[k] = pl.add("V", (lambda k: lambda e: e.tensor_scalar(
                        xnr[:, (k % 4) * D:(k % 4 + 1) * D],
                        X[:, k * D:(k + 1) * D],
                        col(MEAN, k), col(CDOL, k), ALU.subtract,
                        ALU.mult))(k),
                        deps=[math_c_g[k // 4], xnr_free[k % 4]])
                    tra_t[k] = pl.add("S", (lambda k: lambda e: e.dma_start(
                        qdTa3[:, :, (k % 8) * P:(k % 8 + 1) * P],
                        xnr[:, (k % 4) * D:(k % 4 + 1) * D],
                        transpose=True))(k),
                        deps=[xn_t[k],
                              prev_qdT_reader if k < 8 else qkv_mm[k - 8]],
                        inc=DMA, sem="TRA%d" % (k % 4))
                    xnr_free[k % 4] = tra_t[k]
                for k in range(4 * g, 4 * g + 4):
                    slot = k % 4
                    tok = None
                    dd0 = [tra_t[k], wqkvT_done]
                    dd0.append(qkv_ev[k - 4] if k >= 4 else prev_arena_reader)
                    for kc in range(KC):
                        tok = pl.add("T", (lambda k, kc, slot: lambda e:
                            e.matmul(
                                qkv_slots[slot],
                                qdTa3[:, kc, (k % 8) * P:(k % 8 + 1) * P],
                                WqkvT[:, kc * 384:(kc + 1) * 384],
                                start=(kc == 0),
                                stop=(kc == KC - 1)))(k, kc, slot),
                            deps=(dd0 if kc == 0 else []),
                            inc=(1 if kc == KC - 1 else 0))
                    qkv_mm[k] = tok
                for k in range(4 * g, 4 * g + 4):
                    slot = k % 4
                    ev1 = pl.add("V", (lambda k, slot: lambda e: e.tensor_copy(
                        arena[:, A_QK + k * 256: A_QK + (k + 1) * 256],
                        qkv_slots[slot][:, 0:256]))(k, slot),
                        deps=[qkv_mm[k], prev_arena_reader])
                    qkv_ev[k] = pl.add("V", (lambda k, slot: lambda e:
                        e.tensor_copy(
                            arena[:, A_V + k * P: A_V + (k + 1) * P],
                            qkv_slots[slot][:, 256:384]))(k, slot),
                        deps=[ev1])
                    gq_t[0] = pl.add("T", (lambda k: lambda e: e.matmul(
                        psC[:, 0:256],
                        arena[:, A_QK + k * 256: A_QK + k * 256 + P],
                        arena[:, A_QK + k * 256: A_QK + (k + 1) * 256],
                        start=(k == 0), stop=(k == NT - 1)))(k),
                        deps=([ev1, et_ev] if k == 0 else [ev1]),
                        inc=(1 if k == NT - 1 else 0))
                    gk_t[0] = pl.add("T", (lambda k: lambda e: e.matmul(
                        psC[:, 512:640],
                        arena[:, A_QK + k * 256 + P: A_QK + (k + 1) * 256],
                        arena[:, A_QK + k * 256 + P: A_QK + (k + 1) * 256],
                        start=(k == 0), stop=(k == NT - 1)))(k),
                        deps=[], inc=(1 if k == NT - 1 else 0))
                    vtr_t.append(pl.add("Y", (lambda k: lambda e: e.dma_start(
                        arena[:, A_VF + k * P: A_VF + (k + 1) * P],
                        arena[:, A_V + k * P: A_V + (k + 1) * P],
                        transpose=True))(k), deps=[qkv_ev[k]], inc=DMA,
                        sem="VTR"))

            gq_t = [None]
            gk_t = [None]
            for g in range(4):
                aph1_emit(g)
                amath_emit(g)
                aqkv_emit(g)
            gq_tok = gq_t[0]
            gk_tok = gk_t[0]
            vtr_all = [vtr_t[-1]]

            gq_ev = pl.add("V", lambda e: e.tensor_copy(
                Gq[:, :], psC[:, 0:256]), deps=[gq_tok])
            gk_ev = pl.add("V", lambda e: e.tensor_copy(
                Gkk[:, :], psC[:, 512:640]), deps=[gk_tok])
            vtr_all = [vtr_t[-1]]

            DQ2, DK2, NQ, NK, RQ, RK, DQ, RM, RS, RDEN = \
                0, 1, 2, 3, 4, 5, 6, 7, 9, 10

            def sv(i):
                return smallv[:, i:i + 1]

            t = pl.add("V", lambda e: e.tensor_tensor(
                sim1[:, :], Gq[:, 0:P], identf[:, :], ALU.mult),
                deps=[gq_ev, identf_t])
            t = pl.add("V", lambda e: e.reduce_sum(sv(DQ2), sim1[:, :], AX),
                       deps=[t])
            t = pl.add("S", lambda e: e.activation(sv(NQ), sv(DQ2), ACT.Sqrt),
                       deps=[t])
            t = pl.add("V", lambda e: e.tensor_scalar(
                sv(NQ), sv(NQ), L2_EPS, None, ALU.max), deps=[t])
            t = pl.add("V", lambda e: e.reciprocal(sv(RQ), sv(NQ)), deps=[t])
            dq_tok = pl.add("V", lambda e: e.tensor_tensor(
                sv(DQ), sv(RQ), smallv[:, ET:ET + 1], ALU.mult),
                deps=[t, et_col])

            t = pl.add("V", lambda e: e.tensor_tensor(
                sim2[:, :], Gkk[:, :], identf[:, :], ALU.mult),
                deps=[gk_ev, identf_t])
            t = pl.add("V", lambda e: e.reduce_sum(sv(DK2), sim2[:, :], AX),
                       deps=[t])
            t = pl.add("S", lambda e: e.activation(sv(NK), sv(DK2), ACT.Sqrt),
                       deps=[t])
            t = pl.add("V", lambda e: e.tensor_scalar(
                sv(NK), sv(NK), L2_EPS, None, ALU.max), deps=[t])
            rk_tok = pl.add("V", lambda e: e.reciprocal(sv(RK), sv(NK)),
                            deps=[t])
            t = pl.add("T", lambda e: e.transpose(
                ps_tf[0:1, 0:P], smallv[:, RK:RK + 1], identf[:, :]),
                deps=[rk_tok, woutT_done])
            t = pl.add("V", lambda e: e.tensor_copy(
                dkb[0:1, 0:P], ps_tf[0:1, 0:P]), deps=[t])
            t = pl.add("T", lambda e: e.matmul(
                psC[:, 0:P], onesh[0:1, :], dkb[0:1, 0:P],
                start=True, stop=True), deps=[t, onesf_t, gq_ev])
            dkb_tok = pl.add("V", lambda e: e.tensor_copy(
                dkb[:, :], psC[:, 0:P]), deps=[t])

            t = pl.add("V", lambda e: e.tensor_scalar(
                sim1[:, :], Gq[:, P:256], sv(DQ), None, ALU.mult),
                deps=[gq_ev, dq_tok])
            t = pl.add("V", lambda e: e.tensor_tensor(
                sim2[:, :], sim1[:, :], dkb[:, :], ALU.mult),
                deps=[t, dkb_tok])
            t = pl.add("V", lambda e: e.tensor_tensor(
                sim2[:, :], sim2[:, :], maskb[:, :], ALU.add),
                deps=[t, mask_t])
            t_rm = pl.add("V", lambda e: e.reduce_max(sv(RM), sim2[:, :], AX),
                          deps=[t])
            t_rm = pl.add("V", lambda e: e.tensor_scalar(
                sv(RM), sv(RM), -1.0, None, ALU.mult), deps=[t_rm])
            t = pl.add("S", lambda e: e.activation(
                sim1[:, :], sim2[:, :], ACT.Exp, bias=sv(RM), scale=1.0,
                accum_out=sv(RS)), deps=[t_rm])
            t = pl.add("V", lambda e: e.reciprocal(sv(RDEN), sv(RS)), deps=[t])
            p_tok = pl.add("V", lambda e: e.tensor_scalar(
                Pbf[:, :], sim1[:, :], sv(RDEN), None, ALU.mult), deps=[t])
            t = pl.add("T", lambda e: e.transpose(
                ps_tb[:, 0:P], Pbf[:, :], identb[:, :]),
                deps=[p_tok, wfft_done])
            pt_tok = pl.add("V", lambda e: e.tensor_copy(
                PTb[:, :], ps_tb[:, 0:P]), deps=[t])

            av_banks = [psC[:, 0:512], psC[:, 512:1024]]
            av_ev = []
            bank_free = [dkb_tok, gk_ev]
            for g in range(4):
                bk = g % 2
                tok = None
                for j in range(4):
                    tc = 4 * g + j
                    tok = pl.add("T", (lambda tc, bk, j: lambda e: e.matmul(
                        av_banks[bk][:, j * P:(j + 1) * P], PTb[:, :],
                        arena[:, A_VF + tc * P: A_VF + (tc + 1) * P],
                        start=(j == 0), stop=(j == 3)))(tc, bk, j),
                        deps=(([pt_tok, bank_free[bk]] + vtr_all)
                              if j == 0 else []),
                        inc=(1 if j == 3 else 0))
                ev = pl.add("S", (lambda g, bk: lambda e: e.activation(
                    attn_fm[:, g * 512:(g + 1) * 512], av_banks[bk],
                    ACT.Identity))(g, bk), deps=[tok])
                av_ev.append(ev)
                bank_free[bk] = ev

            xplus = [None] * NT
            op_slots = [psA[:, :], psC[:, :]]
            slot_free = [None, Tok("S", pl.counts["S"])]
            for k in range(NT):
                sl = k % 2
                dd = [av_ev[k // 4], woutT_done, slot_free[sl]]
                pl.add("T", (lambda k, sl: lambda e: e.matmul(
                    op_slots[sl][:, 0:512], attn_fm[:, k * P:(k + 1) * P],
                    WoutT[:, 0:512], start=True, stop=True))(k, sl),
                    deps=dd, inc=0)
                t2 = pl.add("T", (lambda k, sl: lambda e: e.matmul(
                    op_slots[sl][:, 512:1024], attn_fm[:, k * P:(k + 1) * P],
                    WoutT[:, 512:1024], start=True, stop=True))(k, sl),
                    deps=[], inc=1)
                xplus[k] = pl.add("V", (lambda k, sl: lambda e:
                    e.scalar_tensor_tensor(
                        X[:, k * D:(k + 1) * D], op_slots[sl],
                        0.0, X[:, k * D:(k + 1) * D], ALU.add, ALU.add,
                        accum_out=col(SUMS, k)))(k, sl),
                    deps=[t2, sq_t[k], xn_t[k]])
                slot_free[sl] = xplus[k]
            return dict(
                xplus=xplus,
                qdT_reader=Tok("T", pl.counts["T"], "T"),
                arena_reader=Tok("T", pl.counts["T"], "T"),
                vscr_user=vscr_user,
                sq_t=sq_t,
                tra_last=[tra_t[12], tra_t[13], tra_t[14], tra_t[15]],
            )

        final_x = xin
        if n_sub >= 1:
            xs_reader1 = [None] * NT
            r1 = ffn(1, xin, xs_reader1, None, woutT_done,
                     Tok("T", 0, "T"), wfft_done,
                     (None, None, None), None,
                     [None, None])
            final_x = r1["xplus"]
        if n_sub >= 2:
            xs_reader2 = [None] * NT
            r2 = attn(r1["xplus"], xs_reader2, r1["qdT_reader"],
                      r1["arena_reader"], r1["vscr_user"],
                      [r1["qdr_free"][0], r1["qdr_free"][1],
                       r1["qdr_free"][2], r1["qdr_free"][3]])
            final_x = r2["xplus"]
        if n_sub >= 3:
            xs_reader3 = [None] * NT
            r3 = ffn(2, r2["xplus"], xs_reader3, r2["xplus"], r1["etm_reader"],
                     r2["qdT_reader"], r2["arena_reader"], r2["vscr_user"],
                     r1["q8_free"],
                     r2["tra_last"])
            final_x = r3["xplus"]

        out_toks = []
        for k in range(NT):
            out_toks.append(pl.add("Y", (lambda k: lambda e: e.dma_start(
                out_d[k * P:(k + 1) * P, :], X[:, k * D:(k + 1) * D]))(k),
                deps=[final_x[k], woutT_done, wfft_done, et_col, wsh_t, bh_t],
                inc=DMA, sem="OUT"))

        if debug:
            for nm, tens, width in (("stats", stats, 15 * NT),
                                    ("q8r", q8r, 2 * D), ("qdr", qdr, 2 * D),
                                    ("qdTa", qdTa, KC * 1024),
                                    ("arena", arena, NOB * T),
                                    ("e_tm", e_tm, NT * D),
                                    ("WffT", WffT, KC * D),
                                    ("cols", cols, 40)):
                dt = tens[:].tensor.dtype
                dbg_d[nm] = nc.dram_tensor("dbg_" + nm, [P, width], dt,
                                           kind="ExternalOutput").ap()
                deps = [final_x[NT - 1],
                        Tok("V", pl.counts["V"]), Tok("S", pl.counts["S"]),
                        Tok("G", pl.counts["G"]), Tok("T", pl.counts["T"])]
                deps += [Tok(sm, pl.counts[sm]) for sm in DMA_SEMS
                         if sm != "OUT" and pl.counts[sm]]
                pl.add("Y", (lambda nm, tens, width: lambda e: e.dma_start(
                    dbg_d[nm], tens[:, 0:width]))(nm, tens, width),
                    deps=deps, inc=DMA, sem="OUT")

        final_out = pl.counts["OUT"]

        @block.sync
        def _(sync):
            pl.emit("Y", sync, sems)
            sync.wait_ge(sems["OUT"], final_out)

        @block.tensor
        def _(tensor):
            pl.emit("T", tensor, sems)

        @block.vector
        def _(vector):
            pl.emit("V", vector, sems)

        @block.scalar
        def _(scalar):
            pl.emit("S", scalar, sems)

        @block.gpsimd
        def _(gpsimd):
            pl.emit("G", gpsimd, sems)

    return nc


def kernel(L0_attn_x, L0_attn_norm_weight, L0_attn_w_qkv, L0_attn_temperature,
           L0_attn_w_out, L0_ffn_weight_q, L0_ffn_weight_scale, L0_ffn_bias):
    x = np.ascontiguousarray(np.asarray(L0_attn_x, dtype=np.float32))
    wq = np.ascontiguousarray(np.asarray(L0_ffn_weight_q, dtype=np.int8))
    ws = np.ascontiguousarray(np.asarray(L0_ffn_weight_scale, dtype=np.float32))
    bias = np.ascontiguousarray(np.asarray(L0_ffn_bias, dtype=np.float32))
    wqkv = np.ascontiguousarray(np.asarray(L0_attn_w_qkv, dtype=np.float32))
    wout = np.ascontiguousarray(np.asarray(L0_attn_w_out, dtype=np.float32))
    nw = np.ascontiguousarray(np.asarray(L0_attn_norm_weight, dtype=np.float32))
    temp = np.ascontiguousarray(
        np.asarray(L0_attn_temperature, dtype=np.float32).reshape(8))

    ident = np.eye(P, dtype=np.float32)
    hb = (np.arange(P) // 16)
    mask = np.where(hb[:, None] == hb[None, :], 0.0,
                    -60000.0).astype(np.float16)
    auxf = np.ascontiguousarray(ident)
    auxb = np.ascontiguousarray(
        np.concatenate([ident.astype(np.float16), mask], axis=1))

    wqT = np.ascontiguousarray(
        wq.T.reshape(8, 128, 1024).transpose(1, 0, 2).reshape(128, 8192))
    cols_in = np.ascontiguousarray(np.concatenate(
        [ws.reshape(8, 128).T, bias.reshape(8, 128).T,
         nw.reshape(8, 128).T], axis=1))
    wqkvT = np.ascontiguousarray(
        wqkv.T.reshape(8, 128, 384).transpose(1, 0, 2).reshape(128, 3072))
    woutT = np.ascontiguousarray(wout.T)

    nc = build_program()
    in_maps = []
    for b in range(8):
        in_maps.append({
            "x": np.ascontiguousarray(x[b]),
            "wqT": wqT, "cols_in": cols_in,
            "wqkvT": wqkvT, "woutT": woutT, "temp": temp,
            "auxf": auxf, "auxb": auxb,
        })
    res = run_bass_kernel_spmd(nc, in_maps, core_ids=list(range(8)))
    out = np.stack([np.asarray(r["out"], dtype=np.float32)
                    for r in res.results], axis=0)
    return out


# version 32
# speedup vs baseline: 1.2164x; 1.0031x over previous
import numpy as np
import ml_dtypes
from contextlib import ExitStack

import concourse.bass as bass
import concourse.mybir as mybir
from concourse.bass_utils import run_bass_kernel_spmd

F32 = mybir.dt.float32
BF16 = mybir.dt.bfloat16
F16 = mybir.dt.float16
I8 = mybir.dt.int8
ALU = mybir.AluOpType
ACT = mybir.ActivationFunctionType
AX = mybir.AxisListType.X

P = 128
NT = 16
D = 1024
KC = 8
NOB = 8
T = NT * P
MAGIC = 12582912.0
LN_EPS = 1e-5
RMS_EPS = 1e-6
Q_EPS = 1e-6
L2_EPS = 1e-12

ENGINES = ("Y", "T", "V", "S", "G")
DMA_SEMS = tuple("XIN%d" % i for i in range(16)) + (
    "WQ", "WKV", "SM", "TRA0", "TRA1", "TRA2", "TRA3", "TRB0", "TRB1",
    "TRB2", "TRB3", "VTR", "OUT")
ALL_SEMS = ENGINES[1:] + DMA_SEMS


class Tok:
    __slots__ = ("sem", "cnt", "stream")

    def __init__(self, sem, cnt, stream=None):
        self.sem = sem
        self.cnt = cnt
        self.stream = stream


class Planner:
    def __init__(self):
        self.items = {e: [] for e in ENGINES}
        self.counts = {s: 0 for s in ALL_SEMS}
        self._clock = {e: {s: 0 for s in ALL_SEMS} for e in ENGINES}
        self._snap = {s: {0: {t: 0 for t in ALL_SEMS}} for s in ALL_SEMS}
        self.n_waits = 0

    def add(self, eng, builder, deps=(), inc=1, sem=None):
        deps = [d for d in deps if d is not None]
        if sem is None:
            sem = eng
        cl = self._clock[eng]
        waits = []
        for d in deps:
            assert isinstance(d, Tok), d
            assert d.cnt <= self.counts[d.sem], "dep on future token"
            if cl.get(d.sem, 0) >= d.cnt:
                continue
            waits.append((d.sem, d.cnt))
            snap = self._best_snap(d.sem, d.cnt)
            for f in ALL_SEMS:
                if snap[f] > cl[f]:
                    cl[f] = snap[f]
            cl[d.sem] = max(cl[d.sem], d.cnt)
        self.n_waits += len(waits)
        self.items[eng].append((waits, builder, sem if inc else None, inc))
        if inc:
            self.counts[sem] += inc
            self._snap[sem][self.counts[sem]] = dict(cl)
        return Tok(sem, self.counts[sem], eng)

    def _best_snap(self, sem, cnt):
        snaps = self._snap[sem]
        if cnt in snaps:
            return snaps[cnt]
        best = 0
        for c in snaps:
            if best < c <= cnt:
                best = c
        return snaps[best]

    def emit(self, eng_name, eng_obj, sems):
        for waits, builder, inc_sem, inc in self.items[eng_name]:
            for (ws, wc) in waits:
                eng_obj.wait_ge(sems[ws], wc)
            inst = builder(eng_obj)
            if inc:
                inst.then_inc(sems[inc_sem], inc)


def build_program(n_sub=3, parts=('vec', 'et', 'wff', 'wqkv', 'wout'), debug=False):
    nc = bass.Bass("TRN2", target_bir_lowering=False,
                   detect_race_conditions=False)

    x_in = nc.dram_tensor("x", [T, D], F32, kind="ExternalInput").ap()
    wqT_in = nc.dram_tensor("wqT", [P, KC * D], I8,
                            kind="ExternalInput").ap()
    cols_in = nc.dram_tensor("cols_in", [P, 24], F32, kind="ExternalInput").ap()
    wqkvT_in = nc.dram_tensor("wqkvT", [P, KC * 384], F32, kind="ExternalInput").ap()
    woutT_in = nc.dram_tensor("woutT", [P, D], F32, kind="ExternalInput").ap()
    temp_in = nc.dram_tensor("temp", [8], F32, kind="ExternalInput").ap()
    auxf_in = nc.dram_tensor("auxf", [P, P], F32, kind="ExternalInput").ap()
    auxb_in = nc.dram_tensor("auxb", [P, 2 * P], F16, kind="ExternalInput").ap()
    out_d = nc.dram_tensor("out", [T, D], F32, kind="ExternalOutput").ap()
    dbg_d = {}

    with ExitStack() as st:
        def sb(name, shape, dtype):
            return st.enter_context(nc.sbuf_tensor(name, shape, dtype))

        def psalloc(name, shape, dtype=F32):
            return st.enter_context(nc.psum_tensor(name, shape, dtype))

        X = sb("X", [P, NT * D], F32)
        WffT = sb("WffT", [P, KC * D], F16)
        WqkvT = sb("WqkvT", [P, KC * 384], F16)
        WoutT = sb("WoutT", [P, D], F16)
        arena = sb("arena", [P, NOB * T], F16)
        e_tm = sb("e_tm", [P, NT * D], F16)
        qdTa = sb("qdTa", [P, KC * 1536], F16)
        attn_fm = sb("attn_fm", [P, T], F16)
        v_scr = sb("v_scr", [P, 3 * D], F32)
        q8r = sb("q8r", [P, 2 * D], I8)
        qdr = sb("qdr", [P, 4 * D], F16)
        xnr = qdr
        stats = sb("stats", [P, 15 * NT], F32)
        Gq = sb("Gq", [P, 256], F32)
        Gkk = sb("Gkk", [P, P], F32)
        sim1 = sb("sim1", [P, P], F32)
        sim2 = sb("sim2", [P, P], F32)
        Pbf = sb("Pbf", [P, P], F16)
        PTb = sb("PTb", [P, P], F16)
        dkb = sb("dkb", [P, P], F16)
        maskb = sb("maskb", [P, P], F16)
        identb = sb("identb", [P, P], F16)
        identf = sb("identf", [P, P], F32)
        onesh = sb("onesh", [1, P], F16)
        cols = sb("cols", [P, 40], F32)
        smallv = sb("smallv", [P, 16], F32)

        psA = psalloc("psA", [P, 1024], F32)
        psB = psalloc("psB", [P, 1024], F32)
        psC = psalloc("psC", [P, 1024], F32)
        ps_tb = psalloc("ps_tb", [P, 512], F16)
        ps_tf = psalloc("ps_tf", [P, 512], F32)

        sems = {e: st.enter_context(nc.semaphore("sem_" + e))
                for e in ALL_SEMS}
        block = st.enter_context(nc.Block())

        pl = Planner()

        qdTa3 = qdTa[:].rearrange("c (kc t) -> c kc t", kc=KC)
        etm3 = e_tm[:].rearrange("t (tc o) -> t tc o", tc=NT)
        A_QK = 0
        A_V = NT * 256
        A_VF = A_V + NT * P

        SUMS, SUMSQ, MEAN, SCOL, AMAX, INVXS, XSC, TMP1, TMP2, TMP3 = \
            (i * NT for i in range(10))
        CDOL = 10 * NT
        ETCOL = 11 * NT
        MAXC, MINC = AMAX, INVXS
        ACOL, BCOL, EPSC = 12 * NT, 13 * NT, 14 * NT
        C_WS, C_BI, C_NW, C_WSH, C_BH = 0, 8, 16, 24, 32
        DMA = 16

        def col(base, k):
            return stats[:, base + k: base + k + 1]

        def slab_sl(base, s):
            return stats[:, base + 4 * s: base + 4 * (s + 1)]

        temp_ld = pl.add("S", lambda e: e.dma_start(
            stats[0:1, ETCOL:ETCOL + 8],
            temp_in[:].rearrange("(a t) -> a t", a=1)),
            inc=DMA, sem="SM")
        identf_t = pl.add("S", lambda e: e.dma_start(
            identf[:, :], auxf_in[:, 0:P]), inc=DMA, sem="SM")
        mask_t = pl.add("S", lambda e: e.dma_start(
            maskb[:, :], auxb_in[:, P:2 * P]), inc=DMA, sem="SM")
        identb_t = pl.add("S", lambda e: e.dma_start(
            identb[:, :], auxb_in[:, 0:P]), inc=DMA, sem="SM")
        cols_ld = pl.add("S", lambda e: e.dma_start(
            cols[:, C_WS:C_WS + 24], cols_in[:, :]), inc=DMA, sem="SM")
        stg_i8 = e_tm[:, 12288:16384].bitcast(I8)
        stg_f32 = e_tm[:, 6144:12288].bitcast(F32)
        stg_wo = e_tm[:, 4096:6144].bitcast(F32)
        wout_ld = pl.add("S", lambda e: e.dma_start(
            stg_wo[:, 0:1024], woutT_in[:, :]), inc=DMA, sem="SM")
        sm_all = wout_ld
        temp_ld = identf_t = mask_t = identb_t = cols_ld = sm_all
        ws_ld = bi_ld = nw_ld = sm_all

        wq_lds = pl.add("S", lambda e: e.dma_start(
            stg_i8[:, :], wqT_in[:, :]), inc=DMA, sem="WQ")
        wff_cvs = []
        for h in range(2):
            cv = pl.add("V", (lambda h: lambda e: e.tensor_copy(
                WffT[:, h * 4 * D:(h + 1) * 4 * D],
                stg_i8[:, h * 4 * D:(h + 1) * 4 * D]))(h),
                deps=([wq_lds] if h == 0 else []))
            wff_cvs.append(cv)
        wfft_done = wff_cvs[-1]

        wkv_lds = pl.add("S", lambda e: e.dma_start(
            stg_f32[:, :], wqkvT_in[:, :]), inc=DMA, sem="WKV")
        wqkv_cvs = []
        for kc in range(KC):
            cv = pl.add("V", (lambda kc: lambda e: e.tensor_scalar(
                WqkvT[:, kc * 384:(kc + 1) * 384],
                stg_f32[:, kc * 384:(kc + 1) * 384],
                cols[:, C_NW + kc:C_NW + kc + 1], None,
                ALU.mult))(kc),
                deps=([wkv_lds, nw_ld] if kc == 0 else []))
            wqkv_cvs.append(cv)
        wqkvT_done = wqkv_cvs[-1]
        woutT_done = pl.add("V", lambda e: e.tensor_copy(
            WoutT[:, :], stg_wo[:, 0:1024]), deps=[wout_ld])

        xin = []
        for k in range(NT):
            xin.append(pl.add("Y", (lambda k: lambda e: e.dma_start(
                X[:, k * D:(k + 1) * D], x_in[k * P:(k + 1) * P, :]))(k),
                deps=([wq_lds] if k == 4 else []),
                inc=DMA, sem="XIN%d" % k))

        epsc_t = pl.add("V", lambda e: e.memset(
            stats[:, EPSC:EPSC + NT], Q_EPS))
        onesf_t = pl.add("V", lambda e: e.memset(onesh[:, :], 1.0))
        wsh_t = pl.add("V", lambda e: e.tensor_scalar(
            cols[:, C_WSH:C_WSH + 8], cols[:, C_WS:C_WS + 8], 0.5, None,
            ALU.mult), deps=[ws_ld])
        bh_t = pl.add("V", lambda e: e.tensor_scalar(
            cols[:, C_BH:C_BH + 8], cols[:, C_BI:C_BI + 8], 0.5, None,
            ALU.mult), deps=[bi_ld])

        ET = 8
        et_exp = pl.add("S", lambda e: e.activation(
            stats[0:1, ETCOL + 8:ETCOL + 16], stats[0:1, ETCOL:ETCOL + 8],
            ACT.Exp), deps=[temp_ld])
        et_bf = pl.add("V", lambda e: e.tensor_copy(
            Pbf[0:1, 0:8], stats[0:1, ETCOL + 8:ETCOL + 16]),
            deps=[et_exp])
        et_mm = pl.add("T", lambda e: e.matmul(
            psC[:, 0:8], onesh[0:1, :], Pbf[0:1, 0:8], start=True,
            stop=True), deps=[et_bf, onesf_t])
        et_ev = pl.add("V", lambda e: e.tensor_copy(
            stats[:, ETCOL:ETCOL + 8], psC[:, 0:8]), deps=[et_mm])
        et_s1 = pl.add("G", lambda e: e.affine_select(
            stats[:, ETCOL + 8:ETCOL + 16], stats[:, ETCOL:ETCOL + 8],
            [[-16, 8]], ALU.is_ge, 0.0, base=0, channel_multiplier=1),
            deps=[et_ev])
        et_s2 = pl.add("G", lambda e: e.affine_select(
            stats[:, ETCOL:ETCOL + 8], stats[:, ETCOL + 8:ETCOL + 16],
            [[16, 8]], ALU.is_ge, 0.0, base=15, channel_multiplier=-1),
            deps=[et_s1])
        et_col = pl.add("V", lambda e: e.reduce_sum(
            smallv[:, ET:ET + 1], stats[:, ETCOL:ETCOL + 8], AX),
            deps=[et_s2])

        def ffn(sub, x_ready, x_s_reader, sums_ready, prev_etm_reader,
                prev_qdT_reader, prev_arena_reader, init_vscr, init_q8,
                init_qdr):
            sq_t = [None] * NT
            sum_t = [None] * NT
            y_t = [None] * NT
            am_t = [None] * NT
            tra_t = [None] * NT
            vscr_user = list(init_vscr)
            q8_free = [init_q8, init_q8]
            qdr_free = list(init_qdr) + [None] * (4 - len(init_qdr))
            fm_last = [None] * 4
            evac_t = {}
            if sub == 1:
                fm_slots = [psB[:, 0:512], psB[:, 512:1024],
                            psA[:, 0:512], psA[:, 512:1024]]
            else:
                fm_slots = [psB[:, 0:512], psB[:, 512:1024]]
            n_slots = len(fm_slots)
            slot_free = [prev_arena_reader] * n_slots

            def phase1_emit(g):
                for k in range(4 * g, 4 * g + 4):
                    sl = 2
                    deps_sq = [x_ready[k], vscr_user[2]]
                    if sub == 1:
                        sum_t[k] = pl.add("S", (lambda k: lambda e:
                            e.activation(
                                v_scr[:, 2 * D:3 * D], X[:, k * D:(k + 1) * D],
                                ACT.Identity, accum_out=col(SUMS, k)))(k),
                            deps=[x_ready[k], vscr_user[2]])
                        deps_sq.append(sum_t[k])
                    if sums_ready is not None:
                        deps_sq.append(sums_ready[k])
                    sq_t[k] = pl.add("S", (lambda k, sl: lambda e: e.activation(
                        v_scr[:, sl * D:(sl + 1) * D], X[:, k * D:(k + 1) * D],
                        ACT.Square, accum_out=col(SUMSQ, k)))(k, sl),
                        deps=deps_sq)
                    vscr_user[2] = sq_t[k]
                    x_s_reader[k] = sq_t[k]
                    am_t[k] = pl.add("V", (lambda k: lambda e: e.reduce_max(
                        col(MAXC, k), X[:, k * D:(k + 1) * D], AX))(k),
                        deps=[x_ready[k]])
                    y_t[k] = pl.add("V", (lambda k: lambda e: e.tensor_reduce(
                        col(MINC, k), X[:, k * D:(k + 1) * D], AX,
                        ALU.min))(k), deps=[x_ready[k]])

            math2_g = [None] * 4

            def phase2_emit(g):
                def R(base, g=g):
                    return stats[:, base + 4 * g: base + 4 * g + 4]

                ks = range(4 * g, 4 * g + 4)
                d1 = [sq_t[k] for k in ks] + [am_t[k] for k in ks] + \
                    [y_t[k] for k in ks] + \
                    ([sum_t[k] for k in ks] if sub == 1 else [])
                t = pl.add("V", lambda e, R=R: e.tensor_scalar(
                    R(MEAN), R(SUMS), 1.0 / D, None, ALU.mult), deps=d1)
                t = pl.add("V", lambda e, R=R: e.tensor_scalar(
                    R(TMP1), R(SUMSQ), 1.0 / D, None, ALU.mult), deps=[t])
                t = pl.add("S", lambda e, R=R: e.activation(
                    R(TMP2), R(MEAN), ACT.Square), deps=[t])
                t = pl.add("V", lambda e, R=R: e.tensor_tensor(
                    R(TMP1), R(TMP1), R(TMP2), ALU.subtract), deps=[t])
                t = pl.add("V", lambda e, R=R: e.tensor_scalar(
                    R(TMP1), R(TMP1), LN_EPS, None, ALU.add), deps=[t])
                t = pl.add("V", lambda e, R=R: e.reciprocal(R(TMP2), R(TMP1)),
                           deps=[t])
                t = pl.add("S", lambda e, R=R: e.activation(
                    R(SCOL), R(TMP2), ACT.Sqrt), deps=[t])
                t = pl.add("V", lambda e, R=R: e.tensor_tensor(
                    R(TMP1), R(MAXC), R(MEAN), ALU.subtract), deps=[t])
                t = pl.add("V", lambda e, R=R: e.tensor_tensor(
                    R(TMP2), R(MEAN), R(MINC), ALU.subtract), deps=[t])
                t = pl.add("V", lambda e, R=R: e.tensor_tensor(
                    R(TMP1), R(TMP1), R(TMP2), ALU.max), deps=[t])
                t = pl.add("V", lambda e, R=R: e.tensor_tensor(
                    R(TMP1), R(TMP1), R(SCOL), ALU.mult), deps=[t])
                t = pl.add("V", lambda e, R=R: e.tensor_tensor(
                    R(TMP1), R(TMP1), R(EPSC), ALU.max), deps=[t, epsc_t])
                t = pl.add("V", lambda e, R=R: e.tensor_scalar(
                    R(XSC), R(TMP1), 1.0 / 127.0, None, ALU.mult), deps=[t])
                t = pl.add("V", lambda e, R=R: e.reciprocal(R(TMP2), R(TMP1)),
                           deps=[t])
                t = pl.add("V", lambda e, R=R: e.tensor_scalar(
                    R(TMP2), R(TMP2), 127.0, None, ALU.mult), deps=[t])
                t = pl.add("V", lambda e, R=R: e.tensor_tensor(
                    R(ACOL), R(SCOL), R(TMP2), ALU.mult), deps=[t])
                t = pl.add("V", lambda e, R=R: e.tensor_tensor(
                    R(BCOL), R(MEAN), R(ACOL), ALU.mult), deps=[t])
                math2_g[g] = pl.add("V", lambda e, R=R: e.tensor_scalar(
                    R(BCOL), R(BCOL), -1.0, None, ALU.mult), deps=[t])

            def quant_emit(s):
                for j in range(4):
                    k = 4 * s + j
                    sl = k % 2
                    sq8 = k % 2
                    t = pl.add("V", (lambda k, sl: lambda e: e.tensor_scalar(
                        v_scr[:, sl * D:(sl + 1) * D], X[:, k * D:(k + 1) * D],
                        col(MEAN, k), col(ACOL, k), ALU.subtract,
                        ALU.mult))(k, sl),
                        deps=[math2_g[s], vscr_user[sl], sq_t[k]])
                    t = pl.add("V", (lambda k, sl, sq8: lambda e: e.tensor_scalar(
                        q8r[:, sq8 * D:(sq8 + 1) * D],
                        v_scr[:, sl * D:(sl + 1) * D],
                        MAGIC, MAGIC, ALU.add, ALU.subtract))(k, sl, sq8),
                        deps=[t, q8_free[sq8]])
                    vscr_user[sl] = t
                    deq = pl.add("S", (lambda k, sq8: lambda e: e.activation(
                        qdr[:, (k % 4) * D:(k % 4 + 1) * D],
                        q8r[:, sq8 * D:(sq8 + 1) * D],
                        ACT.Identity, scale=col(XSC, k)))(k, sq8),
                        deps=[t, qdr_free[k % 4]])
                    q8_free[sq8] = deq
                    slot_dep = prev_qdT_reader if k < 12 else \
                        fm_last[(k - 12) // 4]
                    kslot = (k // 4) % 3 * 4 + k % 4
                    tra_t[k] = pl.add("Y", (lambda k, kslot: lambda e:
                        e.dma_start(
                            qdTa3[:, :, kslot * P:(kslot + 1) * P],
                            qdr[:, (k % 4) * D:(k % 4 + 1) * D],
                            transpose=True))(k, kslot),
                        deps=[deq, slot_dep], inc=DMA,
                        sem="TRA%d" % (k % 4))
                    qdr_free[k % 4] = tra_t[k]

            def fm_emit(s):
                slab_deps = [tra_t[4 * s + j] for j in range(4)] + [wfft_done]
                for ob in range(NOB):
                    slot = (s * NOB + ob) % n_slots
                    dd = slab_deps + [slot_free[slot]]
                    tok = None
                    for kc in range(KC):
                        tok = pl.add("T", (lambda s, ob, kc, slot: lambda e:
                            e.matmul(
                                fm_slots[slot],
                                WffT[:, kc * D + ob * P: kc * D + (ob + 1) * P],
                                qdTa3[:, kc,
                                      (s % 3) * 512:(s % 3) * 512 + 512],
                                start=(kc == 0),
                                stop=(kc == KC - 1)))(s, ob, kc, slot),
                            deps=(dd if kc == 0 else []),
                            inc=(1 if kc == KC - 1 else 0))
                    fm_last[s] = tok
                    ev = pl.add("S", (lambda s, ob, slot: lambda e: e.activation(
                        arena[:, ob * T + s * 512: ob * T + (s + 1) * 512],
                        fm_slots[slot], ACT.Identity,
                        scale=cols[:, C_WSH + ob:C_WSH + ob + 1],
                        bias=cols[:, C_BH + ob:C_BH + ob + 1]))(s, ob, slot),
                        deps=[tok, wsh_t, bh_t, prev_arena_reader])
                    evac_t[(s, ob)] = ev
                    slot_free[slot] = ev

            for g in range(4):
                phase1_emit(g)
            for g in range(4):
                phase2_emit(g)
            trb_t = {}

            def trb_emit(q):
                for ob in range(NOB):
                    dd = [evac_t[(q, ob)], prev_etm_reader]
                    trb_t[(q, ob)] = pl.add("Y", (lambda q, ob: lambda e:
                        e.dma_start(
                            etm3[:, q * 4:(q + 1) * 4, ob * P:(ob + 1) * P],
                            arena[:, ob * T + q * 512: ob * T + (q + 1) * 512],
                            transpose=True))(q, ob), inc=DMA, deps=dd,
                        sem="TRB%d" % q)

            quant_emit(0)
            for g in range(4):
                if g < 3:
                    quant_emit(g + 1)
                fm_emit(g)
                trb_emit(g)

            xplus = [None] * NT
            trb_last = {q: max((trb_t[(q, ob)] for ob in range(NOB)),
                               key=lambda t: t.cnt) for q in range(4)}
            for k in range(NT):
                h = k // 4
                dd = [trb_last[h]]
                dd += [sq_t[k]]
                if sub == 1:
                    xplus[k] = pl.add("V", (lambda k: lambda e:
                        e.scalar_tensor_tensor(
                            X[:, k * D:(k + 1) * D], e_tm[:, k * D:(k + 1) * D],
                            0.0, X[:, k * D:(k + 1) * D], ALU.add, ALU.add,
                            accum_out=col(SUMS, k)))(k), deps=dd)
                else:
                    xplus[k] = pl.add("V", (lambda k: lambda e:
                        e.scalar_tensor_tensor(
                            X[:, k * D:(k + 1) * D], e_tm[:, k * D:(k + 1) * D],
                            0.0, X[:, k * D:(k + 1) * D], ALU.add,
                            ALU.add))(k), deps=dd)
            return dict(
                xplus=xplus,
                etm_reader=xplus[-1],
                qdT_reader=Tok("T", pl.counts["T"], "T"),
                arena_reader=trb_t[(1, NOB - 1)],
                vscr_user=vscr_user,
                q8_free=q8_free[0 if q8_free[0].cnt > q8_free[1].cnt else 1],
                qdr_free=qdr_free,
                tra_last=tra_t[-1],
            )

        def attn(x_ready, x_s_reader, prev_qdT_reader, prev_arena_reader,
                 init_vscr, init_xnr):
            sq_t = [None] * NT
            xn_t = [None] * NT
            tra_t = [None] * NT
            qkv_mm = [None] * NT
            qkv_ev = [None] * NT
            xnr_free = list(init_xnr)
            vscr_user = list(init_vscr)
            qkv_slots = [psB[:, 0:384], psB[:, 512:512 + 384],
                         psA[:, 0:384], psA[:, 512:512 + 384]]
            vtr_t = []

            math_c_g = [None] * 4

            def aph1_emit(g):
                for k in range(4 * g, 4 * g + 4):
                    sl = 2
                    sq_t[k] = pl.add("S", (lambda k, sl: lambda e: e.activation(
                        v_scr[:, sl * D:(sl + 1) * D], X[:, k * D:(k + 1) * D],
                        ACT.Square, accum_out=col(SUMSQ, k)))(k, sl),
                        deps=[x_ready[k], vscr_user[2]])
                    vscr_user[2] = sq_t[k]
                    x_s_reader[k] = sq_t[k]

            def amath_emit(g):
                def R(base, g=g):
                    return stats[:, base + 4 * g: base + 4 * g + 4]

                ks = range(4 * g, 4 * g + 4)
                d1 = [sq_t[k] for k in ks] + [x_ready[k] for k in ks]
                t = pl.add("V", lambda e, R=R: e.tensor_scalar(
                    R(MEAN), R(SUMS), 1.0 / D, None, ALU.mult), deps=d1)
                t = pl.add("V", lambda e, R=R: e.tensor_scalar(
                    R(TMP1), R(SUMSQ), 1.0 / D, None, ALU.mult), deps=[t])
                t = pl.add("S", lambda e, R=R: e.activation(
                    R(TMP2), R(MEAN), ACT.Square), deps=[t])
                t = pl.add("V", lambda e, R=R: e.tensor_tensor(
                    R(TMP1), R(TMP1), R(TMP2), ALU.subtract), deps=[t])
                t = pl.add("V", lambda e, R=R: e.tensor_scalar(
                    R(TMP2), R(TMP1), LN_EPS, None, ALU.add), deps=[t])
                t = pl.add("V", lambda e, R=R: e.reciprocal(R(TMP3), R(TMP2)),
                           deps=[t])
                t_s = pl.add("S", lambda e, R=R: e.activation(
                    R(SCOL), R(TMP3), ACT.Sqrt), deps=[t])
                t = pl.add("V", lambda e, R=R: e.tensor_tensor(
                    R(TMP2), R(TMP1), R(TMP3), ALU.mult), deps=[t_s])
                t = pl.add("V", lambda e, R=R: e.tensor_scalar(
                    R(TMP2), R(TMP2), RMS_EPS, None, ALU.add), deps=[t])
                t = pl.add("V", lambda e, R=R: e.reciprocal(R(TMP1), R(TMP2)),
                           deps=[t])
                t = pl.add("S", lambda e, R=R: e.activation(
                    R(TMP2), R(TMP1), ACT.Sqrt), deps=[t])
                math_c_g[g] = pl.add("V", lambda e, R=R: e.tensor_tensor(
                    R(CDOL), R(SCOL), R(TMP2), ALU.mult), deps=[t])

            def aqkv_emit(g):
                for k in range(4 * g, 4 * g + 4):
                    xn_t[k] = pl.add("V", (lambda k: lambda e: e.tensor_scalar(
                        xnr[:, (k % 4) * D:(k % 4 + 1) * D],
                        X[:, k * D:(k + 1) * D],
                        col(MEAN, k), col(CDOL, k), ALU.subtract,
                        ALU.mult))(k),
                        deps=[math_c_g[k // 4], xnr_free[k % 4]])
                    tra_t[k] = pl.add("S", (lambda k: lambda e: e.dma_start(
                        qdTa3[:, :, (k % 8) * P:(k % 8 + 1) * P],
                        xnr[:, (k % 4) * D:(k % 4 + 1) * D],
                        transpose=True))(k),
                        deps=[xn_t[k],
                              prev_qdT_reader if k < 8 else qkv_mm[k - 8]],
                        inc=DMA, sem="TRA%d" % (k % 4))
                    xnr_free[k % 4] = tra_t[k]
                for k in range(4 * g, 4 * g + 4):
                    slot = k % 4
                    tok = None
                    dd0 = [tra_t[k], wqkvT_done]
                    dd0.append(qkv_ev[k - 4] if k >= 4 else prev_arena_reader)
                    for kc in range(KC):
                        tok = pl.add("T", (lambda k, kc, slot: lambda e:
                            e.matmul(
                                qkv_slots[slot],
                                qdTa3[:, kc, (k % 8) * P:(k % 8 + 1) * P],
                                WqkvT[:, kc * 384:(kc + 1) * 384],
                                start=(kc == 0),
                                stop=(kc == KC - 1)))(k, kc, slot),
                            deps=(dd0 if kc == 0 else []),
                            inc=(1 if kc == KC - 1 else 0))
                    qkv_mm[k] = tok
                for k in range(4 * g, 4 * g + 4):
                    slot = k % 4
                    ev1 = pl.add("V", (lambda k, slot: lambda e: e.tensor_copy(
                        arena[:, A_QK + k * 256: A_QK + (k + 1) * 256],
                        qkv_slots[slot][:, 0:256]))(k, slot),
                        deps=[qkv_mm[k], prev_arena_reader])
                    qkv_ev[k] = pl.add("V", (lambda k, slot: lambda e:
                        e.tensor_copy(
                            arena[:, A_V + k * P: A_V + (k + 1) * P],
                            qkv_slots[slot][:, 256:384]))(k, slot),
                        deps=[ev1])
                    gq_t[0] = pl.add("T", (lambda k: lambda e: e.matmul(
                        psC[:, 0:256],
                        arena[:, A_QK + k * 256: A_QK + k * 256 + P],
                        arena[:, A_QK + k * 256: A_QK + (k + 1) * 256],
                        start=(k == 0), stop=(k == NT - 1)))(k),
                        deps=([ev1, et_ev] if k == 0 else [ev1]),
                        inc=(1 if k == NT - 1 else 0))
                    gk_t[0] = pl.add("T", (lambda k: lambda e: e.matmul(
                        psC[:, 512:640],
                        arena[:, A_QK + k * 256 + P: A_QK + (k + 1) * 256],
                        arena[:, A_QK + k * 256 + P: A_QK + (k + 1) * 256],
                        start=(k == 0), stop=(k == NT - 1)))(k),
                        deps=[], inc=(1 if k == NT - 1 else 0))
                    vtr_t.append(pl.add("Y", (lambda k: lambda e: e.dma_start(
                        arena[:, A_VF + k * P: A_VF + (k + 1) * P],
                        arena[:, A_V + k * P: A_V + (k + 1) * P],
                        transpose=True))(k), deps=[qkv_ev[k]], inc=DMA,
                        sem="VTR"))

            gq_t = [None]
            gk_t = [None]
            for g in range(4):
                aph1_emit(g)
                amath_emit(g)
                aqkv_emit(g)
            gq_tok = gq_t[0]
            gk_tok = gk_t[0]
            vtr_all = [vtr_t[-1]]

            gq_ev = pl.add("V", lambda e: e.tensor_copy(
                Gq[:, :], psC[:, 0:256]), deps=[gq_tok])
            gk_ev = pl.add("V", lambda e: e.tensor_copy(
                Gkk[:, :], psC[:, 512:640]), deps=[gk_tok])
            vtr_all = [vtr_t[-1]]

            DQ2, DK2, NQ, NK, RQ, RK, DQ, RM, RS, RDEN = \
                0, 1, 2, 3, 4, 5, 6, 7, 9, 10

            def sv(i):
                return smallv[:, i:i + 1]

            t = pl.add("V", lambda e: e.tensor_tensor(
                sim1[:, :], Gq[:, 0:P], identf[:, :], ALU.mult),
                deps=[gq_ev, identf_t])
            t = pl.add("V", lambda e: e.reduce_sum(sv(DQ2), sim1[:, :], AX),
                       deps=[t])
            t = pl.add("S", lambda e: e.activation(sv(NQ), sv(DQ2), ACT.Sqrt),
                       deps=[t])
            t = pl.add("V", lambda e: e.tensor_scalar(
                sv(NQ), sv(NQ), L2_EPS, None, ALU.max), deps=[t])
            t = pl.add("V", lambda e: e.reciprocal(sv(RQ), sv(NQ)), deps=[t])
            dq_tok = pl.add("V", lambda e: e.tensor_tensor(
                sv(DQ), sv(RQ), smallv[:, ET:ET + 1], ALU.mult),
                deps=[t, et_col])

            t = pl.add("V", lambda e: e.tensor_tensor(
                sim2[:, :], Gkk[:, :], identf[:, :], ALU.mult),
                deps=[gk_ev, identf_t])
            t = pl.add("V", lambda e: e.reduce_sum(sv(DK2), sim2[:, :], AX),
                       deps=[t])
            t = pl.add("S", lambda e: e.activation(sv(NK), sv(DK2), ACT.Sqrt),
                       deps=[t])
            t = pl.add("V", lambda e: e.tensor_scalar(
                sv(NK), sv(NK), L2_EPS, None, ALU.max), deps=[t])
            rk_tok = pl.add("V", lambda e: e.reciprocal(sv(RK), sv(NK)),
                            deps=[t])
            t = pl.add("T", lambda e: e.transpose(
                ps_tf[0:1, 0:P], smallv[:, RK:RK + 1], identf[:, :]),
                deps=[rk_tok, woutT_done])
            t = pl.add("V", lambda e: e.tensor_copy(
                dkb[0:1, 0:P], ps_tf[0:1, 0:P]), deps=[t])
            t = pl.add("T", lambda e: e.matmul(
                psC[:, 0:P], onesh[0:1, :], dkb[0:1, 0:P],
                start=True, stop=True), deps=[t, onesf_t, gq_ev])
            dkb_tok = pl.add("V", lambda e: e.tensor_copy(
                dkb[:, :], psC[:, 0:P]), deps=[t])

            t = pl.add("V", lambda e: e.tensor_scalar(
                sim1[:, :], Gq[:, P:256], sv(DQ), None, ALU.mult),
                deps=[gq_ev, dq_tok])
            t = pl.add("V", lambda e: e.tensor_tensor(
                sim2[:, :], sim1[:, :], dkb[:, :], ALU.mult),
                deps=[t, dkb_tok])
            t = pl.add("V", lambda e: e.tensor_tensor(
                sim2[:, :], sim2[:, :], maskb[:, :], ALU.add),
                deps=[t, mask_t])
            t_rm = pl.add("V", lambda e: e.reduce_max(sv(RM), sim2[:, :], AX),
                          deps=[t])
            t_rm = pl.add("V", lambda e: e.tensor_scalar(
                sv(RM), sv(RM), -1.0, None, ALU.mult), deps=[t_rm])
            t = pl.add("S", lambda e: e.activation(
                sim1[:, :], sim2[:, :], ACT.Exp, bias=sv(RM), scale=1.0,
                accum_out=sv(RS)), deps=[t_rm])
            t = pl.add("V", lambda e: e.reciprocal(sv(RDEN), sv(RS)), deps=[t])
            p_tok = pl.add("V", lambda e: e.tensor_scalar(
                Pbf[:, :], sim1[:, :], sv(RDEN), None, ALU.mult), deps=[t])
            t = pl.add("T", lambda e: e.transpose(
                ps_tb[:, 0:P], Pbf[:, :], identb[:, :]),
                deps=[p_tok, wfft_done])
            pt_tok = pl.add("V", lambda e: e.tensor_copy(
                PTb[:, :], ps_tb[:, 0:P]), deps=[t])

            av_banks = [psC[:, 0:512], psC[:, 512:1024]]
            av_ev = []
            bank_free = [dkb_tok, gk_ev]
            for g in range(4):
                bk = g % 2
                tok = None
                for j in range(4):
                    tc = 4 * g + j
                    tok = pl.add("T", (lambda tc, bk, j: lambda e: e.matmul(
                        av_banks[bk][:, j * P:(j + 1) * P], PTb[:, :],
                        arena[:, A_VF + tc * P: A_VF + (tc + 1) * P],
                        start=(j == 0), stop=(j == 3)))(tc, bk, j),
                        deps=(([pt_tok, bank_free[bk]] + vtr_all)
                              if j == 0 else []),
                        inc=(1 if j == 3 else 0))
                ev = pl.add("S", (lambda g, bk: lambda e: e.activation(
                    attn_fm[:, g * 512:(g + 1) * 512], av_banks[bk],
                    ACT.Identity))(g, bk), deps=[tok])
                av_ev.append(ev)
                bank_free[bk] = ev

            xplus = [None] * NT
            op_slots = [psA[:, :], psC[:, :], psB[:, :]]
            slot_free = [None, Tok("S", pl.counts["S"]), None]
            for k in range(NT):
                sl = k % 3
                dd = [av_ev[k // 4], woutT_done, slot_free[sl]]
                pl.add("T", (lambda k, sl: lambda e: e.matmul(
                    op_slots[sl][:, 0:512], attn_fm[:, k * P:(k + 1) * P],
                    WoutT[:, 0:512], start=True, stop=True))(k, sl),
                    deps=dd, inc=0)
                t2 = pl.add("T", (lambda k, sl: lambda e: e.matmul(
                    op_slots[sl][:, 512:1024], attn_fm[:, k * P:(k + 1) * P],
                    WoutT[:, 512:1024], start=True, stop=True))(k, sl),
                    deps=[], inc=1)
                xplus[k] = pl.add("V", (lambda k, sl: lambda e:
                    e.scalar_tensor_tensor(
                        X[:, k * D:(k + 1) * D], op_slots[sl],
                        0.0, X[:, k * D:(k + 1) * D], ALU.add, ALU.add,
                        accum_out=col(SUMS, k)))(k, sl),
                    deps=[t2, sq_t[k], xn_t[k]])
                slot_free[sl] = xplus[k]
            return dict(
                xplus=xplus,
                qdT_reader=Tok("T", pl.counts["T"], "T"),
                arena_reader=Tok("T", pl.counts["T"], "T"),
                vscr_user=vscr_user,
                sq_t=sq_t,
                tra_last=[tra_t[12], tra_t[13], tra_t[14], tra_t[15]],
            )

        final_x = xin
        if n_sub >= 1:
            xs_reader1 = [None] * NT
            r1 = ffn(1, xin, xs_reader1, None, woutT_done,
                     Tok("T", 0, "T"), wfft_done,
                     (None, None, None), None,
                     [None, None])
            final_x = r1["xplus"]
        if n_sub >= 2:
            xs_reader2 = [None] * NT
            r2 = attn(r1["xplus"], xs_reader2, r1["qdT_reader"],
                      r1["arena_reader"], r1["vscr_user"],
                      [r1["qdr_free"][0], r1["qdr_free"][1],
                       r1["qdr_free"][2], r1["qdr_free"][3]])
            final_x = r2["xplus"]
        if n_sub >= 3:
            xs_reader3 = [None] * NT
            r3 = ffn(2, r2["xplus"], xs_reader3, r2["xplus"], r1["etm_reader"],
                     r2["qdT_reader"], r2["arena_reader"], r2["vscr_user"],
                     r1["q8_free"],
                     r2["tra_last"])
            final_x = r3["xplus"]

        out_toks = []
        for k in range(NT):
            out_toks.append(pl.add("Y", (lambda k: lambda e: e.dma_start(
                out_d[k * P:(k + 1) * P, :], X[:, k * D:(k + 1) * D]))(k),
                deps=[final_x[k], woutT_done, wfft_done, et_col, wsh_t, bh_t],
                inc=DMA, sem="OUT"))

        if debug:
            for nm, tens, width in (("stats", stats, 15 * NT),
                                    ("q8r", q8r, 2 * D), ("qdr", qdr, 2 * D),
                                    ("qdTa", qdTa, KC * 1024),
                                    ("arena", arena, NOB * T),
                                    ("e_tm", e_tm, NT * D),
                                    ("WffT", WffT, KC * D),
                                    ("cols", cols, 40)):
                dt = tens[:].tensor.dtype
                dbg_d[nm] = nc.dram_tensor("dbg_" + nm, [P, width], dt,
                                           kind="ExternalOutput").ap()
                deps = [final_x[NT - 1],
                        Tok("V", pl.counts["V"]), Tok("S", pl.counts["S"]),
                        Tok("G", pl.counts["G"]), Tok("T", pl.counts["T"])]
                deps += [Tok(sm, pl.counts[sm]) for sm in DMA_SEMS
                         if sm != "OUT" and pl.counts[sm]]
                pl.add("Y", (lambda nm, tens, width: lambda e: e.dma_start(
                    dbg_d[nm], tens[:, 0:width]))(nm, tens, width),
                    deps=deps, inc=DMA, sem="OUT")

        final_out = pl.counts["OUT"]

        @block.sync
        def _(sync):
            pl.emit("Y", sync, sems)
            sync.wait_ge(sems["OUT"], final_out)

        @block.tensor
        def _(tensor):
            pl.emit("T", tensor, sems)

        @block.vector
        def _(vector):
            pl.emit("V", vector, sems)

        @block.scalar
        def _(scalar):
            pl.emit("S", scalar, sems)

        @block.gpsimd
        def _(gpsimd):
            pl.emit("G", gpsimd, sems)

    return nc


def kernel(L0_attn_x, L0_attn_norm_weight, L0_attn_w_qkv, L0_attn_temperature,
           L0_attn_w_out, L0_ffn_weight_q, L0_ffn_weight_scale, L0_ffn_bias):
    x = np.ascontiguousarray(np.asarray(L0_attn_x, dtype=np.float32))
    wq = np.ascontiguousarray(np.asarray(L0_ffn_weight_q, dtype=np.int8))
    ws = np.ascontiguousarray(np.asarray(L0_ffn_weight_scale, dtype=np.float32))
    bias = np.ascontiguousarray(np.asarray(L0_ffn_bias, dtype=np.float32))
    wqkv = np.ascontiguousarray(np.asarray(L0_attn_w_qkv, dtype=np.float32))
    wout = np.ascontiguousarray(np.asarray(L0_attn_w_out, dtype=np.float32))
    nw = np.ascontiguousarray(np.asarray(L0_attn_norm_weight, dtype=np.float32))
    temp = np.ascontiguousarray(
        np.asarray(L0_attn_temperature, dtype=np.float32).reshape(8))

    ident = np.eye(P, dtype=np.float32)
    hb = (np.arange(P) // 16)
    mask = np.where(hb[:, None] == hb[None, :], 0.0,
                    -60000.0).astype(np.float16)
    auxf = np.ascontiguousarray(ident)
    auxb = np.ascontiguousarray(
        np.concatenate([ident.astype(np.float16), mask], axis=1))

    wqT = np.ascontiguousarray(
        wq.T.reshape(8, 128, 1024).transpose(1, 0, 2).reshape(128, 8192))
    cols_in = np.ascontiguousarray(np.concatenate(
        [ws.reshape(8, 128).T, bias.reshape(8, 128).T,
         nw.reshape(8, 128).T], axis=1))
    wqkvT = np.ascontiguousarray(
        wqkv.T.reshape(8, 128, 384).transpose(1, 0, 2).reshape(128, 3072))
    woutT = np.ascontiguousarray(wout.T)

    nc = build_program()
    in_maps = []
    for b in range(8):
        in_maps.append({
            "x": np.ascontiguousarray(x[b]),
            "wqT": wqT, "cols_in": cols_in,
            "wqkvT": wqkvT, "woutT": woutT, "temp": temp,
            "auxf": auxf, "auxb": auxb,
        })
    res = run_bass_kernel_spmd(nc, in_maps, core_ids=list(range(8)))
    out = np.stack([np.asarray(r["out"], dtype=np.float32)
                    for r in res.results], axis=0)
    return out
